# version 23
# baseline (speedup 1.0000x reference)
import numpy as np
import ml_dtypes
import concourse.bass as bass
import concourse.mybir as mybir
from concourse.bass_utils import run_bass_kernel_spmd
from contextlib import ExitStack

F32 = mybir.dt.float32
BF16 = mybir.dt.bfloat16
FP8 = mybir.dt.float8e4
U8 = mybir.dt.uint8
AF = mybir.ActivationFunctionType
ALU = mybir.AluOpType
AX = mybir.AxisListType

D, FF, NDC, NFC, H, MT = 1024, 4096, 8, 32, 16, 128
CW = 256
COLS = 2 * CW
NMT = COLS // MT
EPS = 1e-5
ENGINES = ("sync", "vector", "scalar", "tensor", "gpsimd")


class Sched:
    def __init__(self, nc, es):
        self.nc, self.es = nc, es
        self.ops = {e: [] for e in ENGINES}
        self.sems, self.counts, self.waited = {}, {}, {}
        self.wmax = {}

    def sem(self, name):
        self.sems[name] = self.es.enter_context(self.nc.semaphore(name))
        self.counts[name] = 0
        return name

    def cur(self, s):
        return self.counts[s]

    def step(self, eng, emit, waits=(), inc=None, inc_by=1, name=None):
        wl = []
        for w in waits:
            if isinstance(w, str):
                w = (w, self.counts[w])
            sname, val = w
            if val <= 0:
                continue
            self.wmax[sname] = max(self.wmax.get(sname, 0), val)
            key = (eng, sname)
            if self.waited.get(key, -1) >= val:
                continue
            self.waited[key] = val
            wl.append((self.sems[sname], val))
        handle, n = (self.sems[inc], inc_by) if inc else (None, 0)
        if inc:
            self.counts[inc] += inc_by

        autodrain = eng in ("vector", "scalar")

        def op(e, wl=wl, emit=emit, handle=handle, n=n, name=name, ad=autodrain):
            for hh, v in wl:
                e.wait_ge(hh, v)
            inst = emit(e)
            if handle is not None:
                assert inst is not None, f"step {name}: inc without instruction"
                inst.then_inc(handle, n)
            if ad:
                e.drain()

        self.ops[eng].append(op)
        return self.counts[inc] if inc else None

    def dma(self, eng, out, in_, sem, waits=(), name=None):
        waits = list(waits) + [(sem, self.wmax.get(sem, 0))]
        return self.step(eng, lambda e, o=out, i=in_: e.dma_start(out=o, in_=i),
                         waits=waits, inc=sem, inc_by=16, name=name)

    def fence(self):
        tgts = [(s, self.counts[s]) for s in self.sems]
        for e in ENGINES:
            wl = []
            for sname, val in tgts:
                if val <= 0:
                    continue
                key = (e, sname)
                if self.waited.get(key, -1) >= val:
                    continue
                self.waited[key] = val
                wl.append((self.sems[sname], val))
            if wl:
                self.ops[e].append(lambda en, wl=wl: [en.wait_ge(hh, v) for hh, v in wl])

    def finalize(self, block):
        for e in ENGINES:
            getattr(block, e)(lambda engine, ops=self.ops[e]: [f(engine) for f in ops])


def build(nchunk=16, do_attn=True, attn_upto=3, scan_lvl=4):
    nc = bass.Bass("TRN2", target_bir_lowering=False, debug=False)

    dx = nc.dram_tensor("x", [nchunk, 128, NDC, COLS], F32, kind="ExternalInput").ap()
    dout = nc.dram_tensor("out", [nchunk, 128, NDC, COLS], F32, kind="ExternalOutput").ap()
    dw1 = nc.dram_tensor("w1s", [NFC, 128, NDC * 128], F32, kind="ExternalInput").ap()
    dw2 = nc.dram_tensor("w2s", [NDC, 128, NFC * 128], F32, kind="ExternalInput").ap()
    dwk = nc.dram_tensor("wks", [NDC, 128, NDC * 128], F32, kind="ExternalInput").ap()
    dwr = nc.dram_tensor("wrs", [NDC, 128, NDC * 128], F32, kind="ExternalInput").ap()
    dwg = nc.dram_tensor("wgs", [NDC, 128, NDC * 128], F32, kind="ExternalInput").ap()
    dwv = nc.dram_tensor("wvs", [NDC, 128, D], F32, kind="ExternalInput").ap()
    dwo = nc.dram_tensor("wos", [NDC, 128, NDC * 128], F32, kind="ExternalInput").ap()
    dtm1a = nc.dram_tensor("tm1a", [128, NDC, 128], F32, kind="ExternalInput").ap()
    dtm1b = nc.dram_tensor("tm1b", [128, NDC, 32], F32, kind="ExternalInput").ap()
    dtm2 = nc.dram_tensor("tm2", [128, NDC, 128], F32, kind="ExternalInput").ap()
    dtm24 = nc.dram_tensor("tm24", [32, NDC, 128], F32, kind="ExternalInput").ap()
    dtd1 = nc.dram_tensor("td1", [128, NDC, 64], F32, kind="ExternalInput").ap()
    dtd2 = nc.dram_tensor("td2", [64, NDC, 128], F32, kind="ExternalInput").ap()
    NV = 72
    dvec = nc.dram_tensor("vecs", [128, NV], F32, kind="ExternalInput").ap()
    db1 = nc.dram_tensor("b1c", [128, NFC], F32, kind="ExternalInput").ap()
    dlnw = nc.dram_tensor("lnwc", [128, NDC], F32, kind="ExternalInput").ap()
    dlnb = nc.dram_tensor("lnbc", [128, NDC], F32, kind="ExternalInput").ap()
    db2r = nc.dram_tensor("b2r", [1, NDC * 128], F32, kind="ExternalInput").ap()
    dml = nc.dram_tensor("mask_lower", [128, 128], U8, kind="ExternalInput").ap()
    dmlb = nc.dram_tensor("mask_lower_bf", [128, 128], BF16, kind="ExternalInput").ap()
    dmu = nc.dram_tensor("mask_upper", [128, 128], U8, kind="ExternalInput").ap()
    dmd = nc.dram_tensor("mask_diag", [128, 128], U8, kind="ExternalInput").ap()
    did = nc.dram_tensor("ident", [128, 128], BF16, kind="ExternalInput").ap()
    dbo = nc.dram_tensor("blockones", [128, 2], BF16, kind="ExternalInput").ap()
    dq1 = nc.dram_tensor("q1", [NFC, 128, NDC * 128], FP8, kind="Internal").ap()
    dq2 = nc.dram_tensor("q2", [NDC, 128, NFC * 128], FP8, kind="Internal").ap()
    dbk = nc.dram_tensor("bk", [NDC, 128, NDC * 128], BF16, kind="Internal").ap()
    dbr = nc.dram_tensor("br", [NDC, 128, NDC * 128], BF16, kind="Internal").ap()
    dbg = nc.dram_tensor("bg", [NDC, 128, NDC * 128], BF16, kind="Internal").ap()
    dbv = nc.dram_tensor("bv", [NDC, 128, D], BF16, kind="Internal").ap()
    dbo2 = nc.dram_tensor("bo2", [NDC, 128, NDC * 128], BF16, kind="Internal").ap()

    es = ExitStack()
    S = Sched(nc, es)
    Dm, G, V, A, T = S.sem("Dm"), S.sem("G"), S.sem("V"), S.sem("A"), S.sem("T")
    D1, D2, DP = S.sem("D1"), S.sem("D2"), S.sem("DP")
    D1b, Ds, D1c = S.sem("D1b"), S.sem("Ds"), S.sem("D1c")
    Dq, Dq2 = S.sem("Dq"), S.sem("Dq2")
    sb = lambda name, shape, dt=F32: es.enter_context(nc.sbuf_tensor(name, shape, dt))

    resid = sb("resid", [128, NDC, COLS])
    S_ma = sb("S_ma", [128, 2, NDC, 64])
    carry = sb("carry", [128, NDC, 2], BF16)
    vecs = sb("vecs_s", [128, NV])
    b1c = sb("b1c_s", [128, NFC])
    lnw_c = sb("lnw_s", [128, NDC])
    lnb_c = sb("lnb_s", [128, NDC])
    b2row = sb("b2r_s", [1, NDC, 128], BF16)
    ml_s = sb("ml_s", [128, 128], U8)
    mlb = sb("mlb_s", [128, 128], BF16)
    mu_s = sb("mu_s", [128, 128], U8)
    md_s = sb("md_s", [128, 128], U8)
    id_s = sb("id_s", [128, 128], BF16)
    bo_s = sb("bo_s", [128, 2], BF16)
    zer_s = sb("zer_s", [128, 128], BF16)
    ones_c = sb("ones_c", [128, 1], BF16)
    ones_fc = sb("ones_fc", [128, 1])
    ones_f = sb("ones_f", [1, 128])
    ones2r = sb("ones2r", [1, 512], BF16)
    tm1a = sb("tm1a_s", [128, NDC, 128], BF16)
    tm1b = sb("tm1b_s", [128, NDC, 32], BF16)
    tm2w = sb("tm2w_s", [128, NDC, 128], BF16)
    tm2w4 = sb("tm2w4_s", [32, NDC, 128], BF16)
    td1w = sb("td1w_s", [128, NDC, 64], BF16)
    td2w = sb("td2w_s", [64, NDC, 128], BF16)
    VMAA = {"x": 0, "w": 8, "k": 16, "v": 24, "r": 32, "g": 40}
    VTD, VU, VEPS, VHALF, VNHALF, VRS1, VRS2 = 48, 56, 64, 65, 66, 67, 68

    xln2 = sb("xln2", [128, NDC, 2, CW + 1], BF16)
    sxy = sb("sxy", [128, NDC * COLS * 4], U8)
    def _sview(off, shape, dt):
        isz = {F32: 4, BF16: 2}[dt]
        n = int(np.prod(shape))
        vv = sxy[:, off:off + n * isz].bitcast(dt)
        pat = "p (" + " ".join(f"a{i}" for i in range(len(shape))) + ") -> p " +               " ".join(f"a{i}" for i in range(len(shape)))
        return vv.rearrange(pat, **{f"a{i}": s for i, s in enumerate(shape)})
    stage = _sview(0, [NDC, COLS], F32)
    sx = _sview(0, [NDC, 2, CW], BF16)
    y_ch = _sview(NDC * COLS * 2, [NMT, D], BF16)
    xnlb = sb("xnlb", [128, NDC * (2 * MT + 1) * 4], U8)
    xn = xnlb[:, 0:NDC * COLS * 2].bitcast(BF16).rearrange(
        "p (a b) -> p a b", a=NDC)
    zx = [sb(f"zx{i}", [128, NDC, COLS], BF16) for i in range(2)]
    bigb = sb("bigb", [128, NFC * COLS * 2], U8)
    def _bview(off, shape):
        n = int(np.prod(shape))
        vv = bigb[:, off:off + n * 2].bitcast(BF16)
        pat = "p (" + " ".join(f"a{i}" for i in range(len(shape))) + ") -> p " +               " ".join(f"a{i}" for i in range(len(shape)))
        return vv.rearrange(pat, **{f"a{i}": s for i, s in enumerate(shape)})
    h_sb = _bview(0, [NFC, COLS])
    r_fm = _bview(0, [NDC, COLS])
    k_fm = _bview(NDC * COLS * 2, [NDC, COLS])
    v_tok = _bview(2 * NDC * COLS * 2, [NMT, D])
    g_fm = _bview(3 * NDC * COLS * 2, [NDC, COLS])
    mixh = sb("mixh", [128, COLS], BF16)
    mixh4 = sb("mixh4", [32, COLS], BF16)
    hid = sb("hid", [64, COLS], BF16)
    tz = [sb(f"tz{i}", [128, COLS]) for i in range(2)]
    z3 = sb("z3", [128, NDC, COLS], BF16)
    xb = z3[:, :, :]
    z_tok = sb("z_tok", [128, D], BF16)
    sq_t = sb("sq_t", [128, D], BF16)
    lwp = xnlb[:, 0:NDC * MT * 4].bitcast(F32).rearrange("p (a b) -> p a b", a=NDC)
    Bx = xnlb[:, NDC * MT * 4:NDC * (2 * MT + 1) * 4].bitcast(F32).rearrange(
        "p (a b) -> p a b", a=NDC)
    e2t = sb("e2t_b", [128, NDC, MT])
    e3t = sb("e3t_b", [128, NDC, MT])
    RtX = [sb(f"Rt_{i}", [128, NDC, MT], BF16) for i in range(2)]
    KtX = [sb(f"Kt_{i}", [128, NDC, MT], BF16) for i in range(2)]
    KhFX = [sb(f"KhF_{i}", [128, NDC, MT], BF16) for i in range(2)]
    mrkuX = [sb(f"mrku_{i}", [128, NDC, MT], BF16) for i in range(2)]
    SbfX = [sb(f"Sbf_{i}", [128, NDC, 64], BF16) for i in range(2)]
    Khtok = [sb(f"Khtok{i}", [128, 128], BF16) for i in range(2)]
    Phat = [[sb(f"Phat{i}{p}", [128, 128], BF16) for p in range(2)] for i in range(2)]
    negcX = [sb(f"negc_{i}", [128, 4, NDC]) for i in range(2)]
    negc4 = sb("negc4", [128, NDC])
    negc5X = [sb(f"negc5_{i}", [128, NDC]) for i in range(2)]
    gn_m = sb("gn_m", [128, NMT, H])
    gn_v = sb("gn_v", [128, NMT, H])
    gn_r = sb("gn_r", [128, NMT, H])
    s_m = sb("s_m", [1, 512])
    s_r = sb("s_r", [1, 512])
    xcp2 = sb("xcp2", [128, 2, COLS], BF16)
    sq2 = sb("sq2", [128, 2, COLS], BF16)
    ones_l = sb("ones_l", [128, MT])
    qs = z_tok[:, :]
    qa = sq_t[:, :]
    q8buf = [mixh[:, :].bitcast(FP8), tz[0][:, 0:256].bitcast(FP8)]
    negw = z3[:, :, :].rearrange("p a b -> p (a b)").bitcast(F32)[:, 0:1024]
    wpiece = [zx[i][:, 0:4, :].rearrange("p a b -> p (a b)").bitcast(F32)
              for i in range(2)]
    rw1t = sb("rw1t", [128, 6, NDC, 128], FP8)
    rw1 = [rw1t[:, i, :, :] for i in range(6)]
    rw2 = [sb(f"rw2_{i}", [128, NFC, 128], FP8) for i in range(3)]
    rpj = [sb(f"rpj_{i}", [128, NDC, 128], BF16) for i in range(4)]
    rwv = [sb(f"rwv_{i}", [128, D], BF16) for i in range(2)]

    import os as _os
    if _os.environ.get("SBUF_DBG"):
        from concourse.bass_primitives import MemorySpace as _MS
        print("SBUF allocated:", nc.bytes_allocated(_MS.SBUF))
    pb = [es.enter_context(nc.psum_tensor(f"pb{i}", [128, 512], F32)) for i in range(8)]
    pbf = [p[:, :].bitcast(BF16) for p in pb]

    block = es.enter_context(nc.Block())

    for dst, src in [(vecs[:, :], dvec), (b1c[:, :], db1), (lnw_c[:, :], dlnw),
                     (lnb_c[:, :], dlnb), (ml_s[:, :], dml), (mlb[:, :], dmlb),
                     (mu_s[:, :], dmu), (md_s[:, :], dmd),
                     (id_s[:, :], did), (bo_s[:, :], dbo)]:
        S.dma("sync", dst, src, Dm)
    CON_D = S.cur(Dm)
    S.dma("gpsimd", b2row[:, :, :].rearrange("p a b -> p (a b)"), db2r, G)
    for dst, src in [(dbk, dwk), (dbr, dwr), (dbg, dwg), (dbv, dwv), (dbo2, dwo)]:
        S.dma("gpsimd", dst, src, G)
    for dst, src in [(tm1a[:, :, :], dtm1a), (tm1b[:, :, :], dtm1b),
                     (tm2w[:, :, :], dtm2), (tm2w4[:, :, :], dtm24),
                     (td1w[:, :, :], dtd1), (td2w[:, :, :], dtd2)]:
        S.dma("gpsimd", dst, src, G)
    SMALLW_G = S.cur(G)

    def init_consts(v):
        v.memset(S_ma[:, :, :, :], 0.0)
        for i in range(2):
            for p in range(2):
                v.memset(Phat[i][p][:, :], 0.0)
        v.memset(carry[:, :, :], 0.0)
        v.memset(zer_s[:, :], 0.0)
        v.memset(ones_c[:, :], 1.0)
        v.memset(ones_fc[:, :], 1.0)
        v.memset(ones_f[:, :], 1.0)
        v.memset(ones2r[:, :], 2.0)
        v.memset(Bx[:, :, 0:1], 0.0)
        return v.memset(ones_l[:, :], 1.0)
    S.step("vector", init_consts, inc=V, name="init")
    INIT_V = S.cur(V)

    tq_state = {"st_q": [None, None]}

    def ternarize(dw, dq, nslice, inner, rs_off):
        npp = inner // 1024
        pieces = [(i, j) for i in range(nslice) for j in range(npp)]
        abs_m = [max(S.cur(V), INIT_V)] * 2
        mm_piece = [S.cur(T), S.cur(T)]
        mm_m = 0
        for pi, (i, j) in enumerate(pieces):
            w = wpiece[pi % 2]
            qe, qsem = ("sync", Dm) if pi % 2 == 0 else ("scalar", Ds)
            dmi = S.dma(qe, w[:, :], dw[i][:, j * 1024:(j + 1) * 1024], qsem,
                        waits=[("T", mm_piece[pi % 2]), ("V", abs_m[pi % 2])], name="tw_ld")
            def vabs(v, w=w):
                v.tensor_scalar_mul(negw[:, :], w[:, :], -1.0)
                v.drain()
                return v.tensor_tensor(w[:, :], w[:, :], negw[:, :], ALU.max)
            am = S.step("vector", vabs,
                        waits=[(qsem, dmi), ("T", mm_piece[pi % 2])],
                        inc=V, name="tw_abs")
            abs_m[pi % 2] = am
            for hh in range(2):
                mm_m = S.step("tensor", lambda t, w=w, hh=hh, st=(pi == 0 and hh == 0),
                              sp=(pi == len(pieces) - 1 and hh == 1): t.matmul(
                    pb[6][0:1, :], ones_fc[:, :], w[:, hh * 512:(hh + 1) * 512],
                    start=st, stop=sp), waits=[("V", am)],
                    inc=T, name="tw_sum")
            mm_piece[pi % 2] = mm_m
        def fin(vv, n=float(nslice * inner * 128)):
            vv.tensor_reduce(s_m[0:1, 0:8], pb[6][0:1, :].rearrange("p (a b) -> p a b", a=8),
                             AX.X, ALU.add)
            vv.drain()
            vv.tensor_reduce(s_m[0:1, 8:9], s_m[0:1, 0:8], AX.X, ALU.add)
            vv.drain()
            vv.tensor_scalar(s_m[0:1, 9:10], s_m[0:1, 8:9], 1.0 / n, 1e-8, ALU.mult, ALU.max)
            vv.drain()
            return vv.tensor_scalar_mul(s_m[0:1, 10:11], s_m[0:1, 9:10], -0.5)
        S.step("vector", fin, waits=[("T", mm_m)], inc=V, name="tw_fin")
        S.step("tensor", lambda t: t.matmul(pb[7][:, 0:1], ones_f[:, :], s_m[0:1, 10:11],
                                            start=True, stop=True),
               waits=[("V", S.cur(V))], inc=T, name="tw_bc")
        S.step("vector", lambda v, o=rs_off: v.tensor_copy(vecs[:, o:o + 1], pb[7][:, 0:1]),
               waits=[("T", S.cur(T))], inc=V, name="tw_rs")
        RS = S.cur(V)
        sg_m = [S.cur(A)] * 2
        last_vm = S.cur(V)
        t_p1 = S.cur(T)
        st_q = tq_state["st_q"]
        for pi, (i, j) in enumerate(pieces):
            w = wpiece[pi % 2]
            qe, qsem = ("sync", Dm) if pi % 2 == 0 else ("scalar", Ds)
            dmi = S.dma(qe, w[:, :], dw[i][:, j * 1024:(j + 1) * 1024], qsem,
                        waits=[("A", sg_m[pi % 2]), ("T", t_p1)], name="tq_ld")
            def sgf(s, w=w, o=rs_off):
                s.activation(qa[:, :], w[:, :], AF.Sign, bias=vecs[:, o:o + 1], scale=1.0)
                return s.activation(qs[:, :], w[:, :], AF.Sign, bias=vecs[:, o:o + 1],
                                    scale=-1.0)
            am = S.step("scalar", sgf, waits=[(qsem, dmi), ("V", max(RS, last_vm))],
                        inc=A, name="tq_sign")
            sg_m[pi % 2] = am
            qb = q8buf[pi % 2]
            qsem2 = Dq if pi % 2 == 0 else Dq2
            vm = S.step("vector", lambda v, qb=qb: v.tensor_sub(qb, qa[:, :], qs[:, :]),
                        waits=[("A", am)] +
                              ([st_q[pi % 2]] if st_q[pi % 2] else []),
                        inc=V, name="tq_add")
            last_vm = vm
            smi = S.dma(qe, dq[i][:, j * 1024:(j + 1) * 1024], qb, qsem2,
                        waits=[("V", vm)], name="tq_st")
            st_q[pi % 2] = (qsem2, smi)
    ternarize(dw1, dq1, NFC, NDC * 128, VRS1)
    ternarize(dw2, dq2, NDC, NFC * 128, VRS2)
    S.dma("sync", resid[:, :, :], dx[0], Dm, name="x0")
    S.fence()

    st = {"womm": 0, "ln_pb6": S.cur(A), "ln_pb7": S.cur(V),
          "sq2": [0, 0], "ln_qmm": 0, "kha": [0, 0]}

    def ln_stats_dc(dc, waits=(), on_v=True):
        sl = dc % 2
        if on_v:
            def cpsq_v(v, dc=dc, sl=sl):
                v.tensor_copy(xcp2[:, sl, :], resid[:, dc, :])
                v.drain()
                return v.tensor_mul(sq2[:, sl, :], resid[:, dc, :], resid[:, dc, :])
            am = S.step("vector", cpsq_v,
                        waits=list(waits) + [("T", st["sq2"][sl])], inc=V,
                        name="ln_cpsq_v")
            w = [("V", am)]
        else:
            def cpsq(s, dc=dc, sl=sl):
                s.activation(xcp2[:, sl, :], resid[:, dc, :], AF.Copy)
                return s.activation(sq2[:, sl, :], resid[:, dc, :], AF.Square)
            am = S.step("scalar", cpsq,
                        waits=list(waits) + [("T", st["sq2"][sl])], inc=A, name="ln_cpsq")
            w = [("A", am)]
        if dc == 0:
            w += [("V", st["ln_pb7"])]
        S.step("tensor", lambda t, dc=dc, sl=sl: t.matmul(
            pb[6][0:1, :], ones_c[:, :], xcp2[:, sl, :],
            start=(dc == 0), stop=(dc == NDC - 1)),
            waits=w, inc=T, name="ln_smm")
        t2 = S.step("tensor", lambda t, dc=dc, sl=sl: t.matmul(
            pb[7][0:1, :], ones_c[:, :], sq2[:, sl, :],
            start=(dc == 0), stop=(dc == NDC - 1)),
            waits=[], inc=T, name="ln_qmm")
        st["sq2"][sl] = t2
        st["ln_qmm"] = t2

    def ln_finish(dst_kind):
        def lnvar(v):
            v.tensor_scalar_mul(s_m[0:1, :], pb[6][0:1, :], 1.0 / D)
            v.drain()
            v.tensor_mul(s_r[0:1, :], s_m[0:1, :], s_m[0:1, :])
            v.drain()
            return v.scalar_tensor_tensor(
                s_r[0:1, :], pb[7][0:1, :], 1.0 / D, s_r[0:1, :], ALU.mult, ALU.subtract)
        vv = S.step("vector", lnvar, waits=[("T", st["ln_qmm"])], inc=V, name="ln_var")
        st["ln_pb6"] = st["ln_pb7"] = vv
        def stats2(s):
            s.activation(s_r[0:1, :], s_r[0:1, :], AF.Ln, bias=vecs[0:1, VEPS:VEPS + 1])
            s.drain()
            return s.activation(s_r[0:1, :], s_r[0:1, :], AF.Exp, scale=-0.5)
        ca = S.step("scalar", stats2, waits=[("V", vv)], inc=A, name="ln_rstd")
        S.step("tensor", lambda t: t.matmul(pb[2][:, :], ones_f[:, :], s_m[0:1, :],
                                            start=True, stop=True),
               waits=[("A", ca)], inc=T, name="ln_bm")
        tb = S.step("tensor", lambda t: t.matmul(pb[3][:, :], ones_f[:, :], s_r[0:1, :],
                                                 start=True, stop=True),
                    waits=[], inc=T, name="ln_ba")
        lnv_dc = []
        for h in range(4):
            ds = slice(2 * h, 2 * h + 2)
            S.step("vector", lambda v, ds=ds: v.tensor_tensor(
                xb[:, ds, :], resid[:, ds, :],
                pb[2][:, :].unsqueeze(1).broadcast_to([128, 2, 512]),
                ALU.subtract),
                waits=[("T", tb), ("T", st["womm"])], inc=V, name="ln_sub")
            if dst_kind == "xn":
                vm = S.step("vector", lambda v, ds=ds: v.tensor_tensor(
                    xn[:, ds, :], xb[:, ds, :],
                    pb[3][:, :].unsqueeze(1).broadcast_to([128, 2, 512]),
                    ALU.mult), waits=[], inc=V, name="ln_mul")
            else:
                vm = S.step("vector", lambda v, ds=ds: v.tensor_tensor(
                    xln2[:, ds, :, 1:], xb[:, ds, :].rearrange("p a (b c) -> p a b c", b=2),
                    pb[3][:, :].rearrange("p (b c) -> p b c", b=2).unsqueeze(1)
                    .broadcast_to([128, 2, 2, CW]),
                    ALU.mult), waits=[], inc=V, name="ln_mul")
            lnv_dc += [vm, vm]
        return lnv_dc

    def ffn_app(app, post_d_cb=None):
        lnv_dc = ln_finish("xn")
        gel_bank = [0, 0]
        slot_cons = [S.cur(T)] * 6
        dmis = {}

        def w1dma(f0):
            sl = f0 % 6
            sem = (D1, D1b, D1c)[(f0 // 2) % 3]
            gate = max(slot_cons[sl], slot_cons[sl + 1])
            dmi = S.dma("sync", rw1t[:, sl:sl + 2, :, :],
                        dq1[f0:f0 + 2].transpose([1, 0, 2]), sem,
                        waits=[("A", gate) if f0 >= 6 else ("T", gate)],
                        name="w1dma")
            dmis[f0] = dmis[f0 + 1] = (sem, dmi)
        w1dma(0)
        w1dma(2)
        w1dma(4)
        for f in range(NFC):
            sl = f % 6
            tl = 0
            for dc in range(NDC):
                tl = S.step("tensor", lambda t, sl=sl, dc=dc, f=f: t.matmul(
                    pb[f % 2][:, :], rw1[sl][:, dc, :], xn[:, dc, :],
                    start=(dc == 0), stop=(dc == NDC - 1)),
                    waits=[dmis[f], ("V", lnv_dc[dc]), ("A", gel_bank[f % 2])],
                    inc=T, name="mm1")
            am = S.step("scalar", lambda s, f=f: s.activation(
                h_sb[:, f, :], pb[f % 2][:, :], AF.Gelu, bias=b1c[:, f:f + 1], scale=0.5),
                waits=[("T", tl)], inc=A, name="gelu")
            gel_bank[f % 2] = am
            slot_cons[sl] = am
            if f % 2 == 1 and f + 5 < NFC:
                w1dma(f + 5)
        HD = S.cur(A)
        ev_bank = [0, 0]
        w2_cons = [S.cur(T)] * 3
        out_v = 0
        for d in range(NDC):
            sl = d % 3
            dmi = S.dma("sync", rw2[sl][:, :, :], dq2[d], D2,
                        waits=[("T", w2_cons[sl])], name="w2dma")
            tl = 0
            for f in range(NFC):
                tl = S.step("tensor", lambda t, sl=sl, f=f, d=d: t.matmul(
                    pb[4 + d % 2][:, :], rw2[sl][:, f, :], h_sb[:, f, :],
                    start=(f == 0), stop=False),
                    waits=[("D2", dmi), ("A", HD), ("V", ev_bank[d % 2])], inc=T, name="mm2")
            tl = S.step("tensor", lambda t, d=d: t.matmul(
                pb[4 + d % 2][:, :], b2row[0:1, d, :], ones2r[0:1, :],
                start=False, stop=True), waits=[], inc=T, name="b2mm")
            w2_cons[sl] = tl
            dst = stage if app == "A" else resid
            out_v = S.step("vector", lambda v, d=d, dst=dst: v.scalar_tensor_tensor(
                dst[:, d, :], pb[4 + d % 2][:, :], 0.25, resid[:, d, :],
                ALU.mult, ALU.add), waits=[("T", tl)], inc=V, name="ffn_out")
            ev_bank[d % 2] = out_v
            if post_d_cb is not None:
                post_d_cb(d, out_v)
        return S.cur(V)

    def mix_and_proj():
        S.step("vector", lambda v: v.tensor_copy(
            xln2[:, :, :, 0:1].squeeze(3), carry[:, :, :]),
            waits=["Dm", "G"], inc=V, name="halo")
        ln_finish("xln2")
        pxw = S.cur(T)
        px_v = []
        for h in range(4):
            ds = slice(2 * h, 2 * h + 2)
            S.step("vector", lambda v, ds=ds: v.tensor_tensor(
                sx[:, ds, :, :], xln2[:, ds, :, 0:CW], xln2[:, ds, :, 1:CW + 1],
                ALU.subtract), waits=[], inc=V, name="sx")
            for dc in (2 * h, 2 * h + 1):
                pv = S.step("vector", lambda v, dc=dc: v.scalar_tensor_tensor(
                    zx[0][:, dc, :].rearrange("p (b c) -> p b c", b=2), sx[:, dc, :, :],
                    vecs[:, VMAA["x"] + dc:VMAA["x"] + dc + 1],
                    xln2[:, dc, :, 1:],
                    ALU.mult, ALU.add), waits=[("T", pxw)], inc=V, name="px")
                px_v.append(pv)
        S.step("vector", lambda v: v.tensor_copy(
            carry[:, :, :], xln2[:, :, :, CW:CW + 1].squeeze(3)),
            waits=[], inc=V, name="carry")
        pxv = S.cur(V)
        tl = tlb = 0
        for dc in range(NDC):
            tl = S.step("tensor", lambda t, dc=dc: t.matmul(
                pb[0][:, :], tm1a[:, dc, :], zx[0][:, dc, :],
                start=(dc == 0), stop=(dc == NDC - 1)),
                waits=[("V", px_v[dc]), ("G", SMALLW_G)], inc=T, name="tm1a")
            tlb = S.step("tensor", lambda t, dc=dc: t.matmul(
                pb[1][0:32, :], tm1b[:, dc, :], zx[0][:, dc, :],
                start=(dc == 0), stop=(dc == NDC - 1)),
                waits=[], inc=T, name="tm1b")
        S.step("scalar", lambda s: s.activation(mixh[:, :], pb[0][:, :], AF.Tanh),
               waits=[("T", tlb)], inc=A, name="tmtanh")
        mha = S.step("scalar", lambda s: s.activation(mixh4[:, :], pb[1][0:32, :], AF.Tanh),
                     waits=[], inc=A, name="tmtanh4")

        def build_zx(z, zi, slot, prev_use_T, prev_vm=0):
            vm = 0
            a_gate = max(mha, S.cur(A))
            bank_ev = [prev_vm] * 4
            vm_percb = []
            tmm = {}

            def emit_mz(cb):
                bank = cb % 4
                if zi < 4:
                    lh, rh, tp = tm2w[32 * zi:32 * zi + 32, cb, :], \
                        mixh[32 * zi:32 * zi + 32, :], (32 * zi, 0)
                else:
                    lh, rh, tp = tm2w4[:, cb, :], mixh4[:, :], (0, 0)
                tmm[cb] = S.step("tensor", lambda t, lh=lh, rh=rh, tp=tp, bank=bank: t.matmul(
                    pb[bank][:, :], lh, rh, start=True, stop=True, tile_position=tp),
                    waits=[("A", a_gate), ("V", bank_ev[bank]),
                           ("T", prev_use_T if cb == 0 else 0)],
                    inc=T, name="mz")

            for cb in range(4):
                emit_mz(cb)
            for cb in range(NDC):
                bank = cb % 4
                def mzzx(v, cb=cb, z=z, bank=bank, slot=slot):
                    v.scalar_tensor_tensor(
                        tz[cb % 2][:, :].rearrange("p (b c) -> p b c", b=2),
                        pb[bank][:, :].rearrange("p (b c) -> p b c", b=2),
                        vecs[:, VMAA[z] + cb:VMAA[z] + cb + 1],
                        sx[:, cb, :, :], ALU.add, ALU.mult)
                    v.drain()
                    return v.tensor_tensor(
                        zx[slot][:, cb, :].rearrange("p (b c) -> p b c", b=2),
                        tz[cb % 2][:, :].rearrange("p (b c) -> p b c", b=2),
                        xln2[:, cb, :, 1:], ALU.add)
                vm = S.step("vector", mzzx, waits=[("T", tmm[cb])], inc=V, name="mzzx")
                bank_ev[bank] = vm
                vm_percb.append(vm)
                if cb + 4 < NDC:
                    emit_mz(cb + 4)
            return vm, vm_percb

        zx_T = {0: tlb, 1: 0}
        slot = 1
        proj_slot = [0] * 4
        proj_slot_t = S.cur(T)
        pj_bank = [0, 0]
        prev_vm = 0
        for z, zi in [("w", 0), ("k", 1), ("v", 2), ("r", 3), ("g", 4)]:
            vm, vm_percb = build_zx(z, zi, slot, zx_T[slot], prev_vm)
            prev_vm = vm
            if z == "w":
                tl2 = 0
                for dc in range(NDC):
                    tl2 = S.step("tensor", lambda t, dc=dc, slot=slot: t.matmul(
                        pb[6][0:64, :], td1w[:, dc, :], zx[slot][:, dc, :],
                        start=(dc == 0), stop=(dc == NDC - 1)),
                        waits=[("V", vm_percb[dc])], inc=T, name="td1")
                S.step("scalar", lambda s: s.activation(hid[:, :], pb[6][0:64, :], AF.Tanh),
                       waits=[("T", tl2)], inc=A, name="tdtanh")
                zx_T[slot] = tl2
            elif z == "v":
                lastT = 0
                rwv_cons = [0, 0]
                for dcq in range(NDC):
                    dmi = S.dma("sync", rwv[dcq % 2][:, :], dbv[dcq], DP,
                                waits=[("T", rwv_cons[dcq % 2])], name="wvdma")
                    for mtv in range(NMT):
                        for half in range(2):
                            lastT = S.step("tensor",
                                lambda t, dcq=dcq, mtv=mtv, half=half, slot=slot: t.matmul(
                                    pb[2 * mtv + half][:, :],
                                    zx[slot][:, dcq, mtv * MT:(mtv + 1) * MT],
                                    rwv[dcq % 2][:, half * 512:(half + 1) * 512],
                                    start=(dcq == 0), stop=(dcq == NDC - 1)),
                                waits=[("DP", dmi),
                                       ("V", vm if dcq == 0 else vm_percb[dcq]),
                                       ("A", S.cur(A) if dcq == 0 else 0)],
                                inc=T, name="vproj")
                    rwv_cons[dcq % 2] = lastT
                for mtv in range(NMT):
                    for half in range(2):
                        S.step("scalar", lambda s, mtv=mtv, half=half: s.activation(
                            v_tok[:, mtv, half * 512:(half + 1) * 512],
                            pb[2 * mtv + half][:, :], AF.Copy),
                            waits=[("T", lastT)], inc=A, name="vev")
                zx_T[slot] = lastT
                pj_bank = [S.cur(A)] * 2
            else:
                dst = {"k": k_fm, "r": r_fm, "g": g_fm}[z]
                fn = AF.Silu if z == "g" else AF.Copy
                dsrc = {"k": dbk, "r": dbr, "g": dbg}[z]
                lastT = 0
                for rc in range(NDC):
                    sl = rc % 4
                    dmi = S.dma("sync", rpj[sl][:, :, :], dsrc[rc], DP,
                                waits=[("A", proj_slot[sl]), ("T", proj_slot_t)],
                                name="pjdma")
                    for dc in range(NDC):
                        lastT = S.step("tensor", lambda t, sl=sl, dc=dc, rc=rc, slot=slot:
                            t.matmul(pb[4 + rc % 2][:, :], rpj[sl][:, dc, :], zx[slot][:, dc, :],
                                     start=(dc == 0), stop=(dc == NDC - 1)),
                            waits=[("DP", dmi), ("V", vm_percb[dc]), ("A", pj_bank[rc % 2])],
                            inc=T, name="proj")
                    am = S.step("scalar", lambda s, rc=rc, dst=dst, fn=fn: s.activation(
                        dst[:, rc, :], pb[4 + rc % 2][:, :], fn), waits=[("T", lastT)],
                        inc=A, name="projev")
                    proj_slot[sl] = am
                    pj_bank[rc % 2] = am
                zx_T[slot] = lastT
            slot = 1 - slot
        return S.cur(A)

    def scan(scan_lvl=4):
        dmc = S.cur(Dm)
        dmg = S.cur(G)
        a0 = S.cur(A)
        par_sel = [0, 0]
        par_kv = [0, 0]
        last_ymm = S.cur(T)
        st7 = {"evac": [a0, a0], "lwp_cons": 0, "yev6": a0, "yev7": a0,
               "ymm_pmt": last_ymm}
        chain = {}

        def emit_td2_wave(mt, wave):
            bank = pb[wave]
            t_last = 0
            for i in range(4):
                cc = 4 * wave + i
                t_last = S.step("tensor", lambda t, cc=cc, i=i, mt=mt, bank=bank: t.matmul(
                    bank[:, 128 * i:128 * i + 128], td2w[:, cc, :],
                    hid[:, mt * MT:mt * MT + MT], start=True, stop=True),
                    waits=[("A", st7["evac"][wave]), ("A", a0 if mt == 0 else 0)],
                    inc=T, name="td2")
            def lwpev(s, wave=wave, bank=bank):
                last = None
                for i in range(4):
                    cc = 4 * wave + i
                    last = s.activation(lwp[:, cc, :], bank[:, 128 * i:128 * i + 128],
                                        AF.Exp, bias=vecs[:, VTD + cc:VTD + cc + 1])
                return last
            st7["evac"][wave] = S.step("scalar", lwpev,
                                 waits=[("T", t_last), ("V", st7["lwp_cons"])],
                                 inc=A, name="lwpev")

        def emit_cum(mt):
            def cum(v):
                v.memset(Bx[:, :, 0:1], 0.0)
                for cc in range(NDC):
                    v.tensor_tensor_scan(Bx[:, cc, 1:MT + 1], ones_l[:, :], lwp[:, cc, :],
                                         0.0, ALU.mult, ALU.add)
                v.drain()
                negc = negcX[mt % 2]
                v.tensor_copy(negc[:, 0, :], Bx[:, :, MT // 2:MT // 2 + 1].squeeze(2))
                v.tensor_copy(negc[:, 1, :], Bx[:, :, MT:MT + 1].squeeze(2))
                v.drain()
                v.tensor_tensor(negc4[:, :], negc[:, 1, :], negc[:, 0, :], ALU.subtract)
                v.drain()
                return v.tensor_tensor(Bx[:, :, :], Bx[:, :, :],
                                       negc[:, 0, :].unsqueeze(2)
                                       .broadcast_to([128, NDC, MT + 1]), ALU.subtract)
            return S.step("vector", cum, waits=[("A", max(st7["evac"]))], inc=V,
                          name="cumsum")

        def emit_exps(mt, cv):
            def exps(s):
                negc = negcX[mt % 2]
                s.activation(lwp[:, :, :], Bx[:, :, 0:MT], AF.Exp, scale=-1.0)
                s.activation(e2t[:, :, :], Bx[:, :, 1:MT + 1], AF.Exp)
                s.activation(negc[:, 2, :], negc[:, 0, :], AF.Exp, scale=-1.0)
                s.activation(negc5X[mt % 2][:, :], negc4[:, :], AF.Exp, scale=-1.0)
                return s.activation(negc[:, 3, :], negc[:, 1, :], AF.Exp, scale=-1.0)
            return S.step("scalar", exps, waits=[("V", cv)], inc=A, name="exps")

        def emit_scal(mt, ea):
            alt = mt % 2
            def scal(v, mt=mt, alt=alt):
                m0 = mt * MT
                rsl = r_fm[:, :, m0:m0 + MT]
                ksl = k_fm[:, :, m0:m0 + MT]
                v.tensor_tensor(e3t[:, :, :], e2t[:, :, :],
                                negc5X[alt][:, :].unsqueeze(2).broadcast_to([128, NDC, MT]),
                                ALU.mult)
                v.tensor_tensor(RtX[alt][:, :, :], rsl, lwp[:, :, :], ALU.mult)
                v.tensor_tensor(KtX[alt][:, :, :], ksl, e2t[:, :, :], ALU.mult)
                v.tensor_tensor(mrkuX[alt][:, :, :], rsl, ksl, ALU.mult)
                v.drain()
                v.tensor_tensor(KhFX[alt][:, :, :], ksl, e3t[:, :, :], ALU.mult)
                return v.tensor_tensor(mrkuX[alt][:, :, :], mrkuX[alt][:, :, :],
                                       vecs[:, VU:VU + 8].unsqueeze(2)
                                       .broadcast_to([128, NDC, MT]), ALU.mult)
            sv = S.step("vector", scal, waits=[("A", ea), ("T", st7["ymm_pmt"])],
                        inc=V, name="scal")
            st7["lwp_cons"] = sv
            return sv

        def emit_sbf(mt):
            alt = mt % 2
            b = mt // (NMT // 2)
            return S.step("vector", lambda v, alt=alt, b=b: v.tensor_tensor(
                SbfX[alt][:, :, :], S_ma[:, b, :, :],
                negcX[alt][:, 2, :].unsqueeze(2).broadcast_to([128, NDC, 64]), ALU.mult),
                waits=[("T", st7["ymm_pmt"])], inc=V, name="sbf")

        emit_td2_wave(0, 0)
        emit_td2_wave(0, 1)
        cv = emit_cum(0)
        ea = emit_exps(0, cv)
        chain[0] = (emit_scal(0, ea), 0)
        chain[0] = (chain[0][0], emit_sbf(0))

        for mt in range(NMT):
            alt = mt % 2
            b = mt // (NMT // 2)
            Rt, Kt, KhF, mrku, Sbf = RtX[alt], KtX[alt], KhFX[alt], mrkuX[alt], SbfX[alt]
            negc = negcX[alt]
            sv_m, sbf_m = chain[mt]
            lastY = 0
            tp_done = {}
            st7["ymm_pmt"] = last_ymm

            def emit_pmm(cc, Rt=Rt, Kt=Kt, KhF=KhF, mrku=mrku, sv_m=sv_m):
                par = cc % 2
                PB, PK, PKb = pb[2 + par], pb[4 + par], pbf[4 + par]
                def pmm2(t, cc=cc, PB=PB, PK=PK, PKb=PKb):
                    t.matmul(PB[:, 128:256], Kt[0:64, cc, :], Rt[0:64, cc, :],
                             start=True, stop=True)
                    t.matmul(PK[:, 0:128], Kt[64:128, cc, :], Rt[64:128, cc, :],
                             start=True, stop=True)
                    t.matmul(PB[:, 384:386], mrku[:, cc, :], bo_s[:, :],
                             start=True, stop=True)
                    return t.transpose(PKb[:, 768:896], KhF[:, cc, :], id_s[:, :])
                tp_done[cc] = S.step("tensor", pmm2,
                                     waits=[("V", sv_m), ("V", par_kv[par]),
                                            ("V", par_sel[par])],
                                     inc=T, name="pmm")

            emit_pmm(0)
            pend_supd = None
            for cc in range(NDC):
                par = cc % 2
                PB, PK, PKb = pb[2 + par], pb[4 + par], pbf[4 + par]
                tp = tp_done[cc]
                def sel(v, par=par, PB=PB, PK=PK, PKb=PKb):
                    for p in range(2):
                        psrc = PB[:, 128:256] if p == 0 else PK[:, 0:128]
                        v.copy_predicated(Phat[par][p][:, :], ml_s[:, :], psrc)
                        v.drain()
                        v.copy_predicated(Phat[par][p][:, :], md_s[:, :],
                                          PB[:, 384 + p:385 + p].broadcast_to([128, 128]))
                        v.drain()
                    return v.tensor_copy(Khtok[par][:, :], PKb[:, 768:896])
                sm = S.step("vector", sel, waits=[("T", tp)], inc=V, name="sel")
                par_sel[par] = sm
                if pend_supd is not None:
                    pend_supd()
                if cc + 1 < NDC:
                    emit_pmm(cc + 1)
                def ymm(t, cc=cc, par=par, PK=PK, mt=mt, Rt=Rt, Sbf=Sbf):
                    for p in range(2):
                        hh = 2 * cc + p
                        ysl = pb[6 + hh // 8][:, 64 * (hh % 8):64 * (hh % 8) + 64]
                        vsl = v_tok[:, mt, 64 * hh:64 * hh + 64]
                        t.matmul(ysl, Phat[par][p][:, :], vsl, start=True, stop=False)
                        t.matmul(ysl, Rt[64 * p:64 * p + 64, cc, :],
                                 Sbf[64 * p:64 * p + 64, cc, :], start=False, stop=True)
                    last = None
                    for p in range(2):
                        vsl = v_tok[:, mt, 64 * (2 * cc + p):64 * (2 * cc + p) + 64]
                        last = t.matmul(PK[64 * p:64 * p + 64, 448:512],
                                        Khtok[par][:, 64 * p:64 * p + 64], vsl,
                                        start=True, stop=True, tile_position=(0, 64 * p))
                    return last
                ty = S.step("tensor", ymm,
                            waits=[("V", sm), ("V", sbf_m),
                                   ("A", st7["yev6"] if cc == 0 else
                                    (st7["yev7"] if cc == 4 else 0))],
                            inc=T, name="ymm")
                lastY = ty
                last_ymm = ty
                def mk_supd(par=par, PK=PK, b=b, cc=cc, negc=negc, ty=ty):
                    def go():
                        kvv = S.step("vector",
                            lambda v: v.scalar_tensor_tensor(
                                S_ma[:, b, cc, :], S_ma[:, b, cc, :],
                                negc[:, 3, cc:cc + 1],
                                PK[:, 448:512], ALU.mult, ALU.add),
                            waits=[("T", ty)], inc=V, name="supd")
                        par_kv[par] = kvv
                    return go
                pend_supd = mk_supd()
                if cc == 3:
                    st7["yev6"] = S.step("scalar", lambda s, mt=mt: s.activation(
                        y_ch[:, mt, 0:512], pb[6][:, :], AF.Copy, scale=0.125),
                        waits=[("T", ty), ("Dm", dmc), ("G", dmg)], inc=A, name="yev6")
                if mt + 1 < NMT:
                    if cc == 1:
                        emit_td2_wave(mt + 1, 0)
                    elif cc == 3:
                        emit_td2_wave(mt + 1, 1)
                    elif cc == 4:
                        nxt_cv = emit_cum(mt + 1)
                    elif cc == 5:
                        nxt_ea = emit_exps(mt + 1, nxt_cv)
                    elif cc == 6:
                        nxt_sv = emit_scal(mt + 1, nxt_ea)
            pend_supd()
            if mt + 1 < NMT:
                chain[mt + 1] = (nxt_sv, emit_sbf(mt + 1))
            st7["yev7"] = S.step("scalar", lambda s, mt=mt: s.activation(
                y_ch[:, mt, 512:1024], pb[7][:, :], AF.Copy, scale=0.125),
                waits=[("T", lastY), ("Dm", dmc), ("G", dmg)], inc=A, name="yev")
        def gns_all(v):
            for mt in range(NMT):
                yv = y_ch[:, mt, :].rearrange("p (h k) -> p h k", h=H)
                v.tensor_reduce(gn_m[:, mt, :], yv, AX.X, ALU.add)
                v.tensor_mul(sq_t[:, :], y_ch[:, mt, :], y_ch[:, mt, :])
                v.drain()
                v.tensor_reduce(gn_v[:, mt, :],
                                sq_t[:, :].rearrange("p (h k) -> p h k", h=H),
                                AX.X, ALU.add)
                v.drain()
            return v.engine_nop()
        S.step("vector", gns_all, waits=[("A", st7["yev7"])], inc=V, name="gnstats")
        return S.cur(V)

    def gn_z_wo():
        def gfin(v):
            v.tensor_scalar_mul(gn_m[:, :, :], gn_m[:, :, :], 1.0 / 64)
            v.drain()
            v.tensor_mul(gn_r[:, :, :], gn_m[:, :, :], gn_m[:, :, :])
            v.drain()
            return v.scalar_tensor_tensor(gn_v[:, :, :], gn_v[:, :, :], 1.0 / 64,
                                          gn_r[:, :, :], ALU.mult, ALU.subtract)
        S.step("vector", gfin, waits=["V"], inc=V, name="gnfin")
        S.step("scalar", lambda s: s.activation(gn_r[:, :, :], gn_v[:, :, :], AF.Ln,
                                                bias=vecs[:, VEPS:VEPS + 1]),
               waits=[("V", S.cur(V))], inc=A, name="gnln")
        def _gnexp(s):
            s.drain()
            return s.activation(gn_r[:, :, :], gn_r[:, :, :], AF.Exp, scale=-0.5)
        ga = S.step("scalar", _gnexp, waits=[], inc=A, name="gnexp")
        tr_bank = [0, 0]
        last_zev = 0
        for mt in range(NMT):
            def nrm(v, mt=mt):
                yv = y_ch[:, mt, :].rearrange("p (h k) -> p h k", h=H)
                mb = gn_m[:, mt, :].unsqueeze(2).broadcast_to([128, H, 64])
                rb = gn_r[:, mt, :].unsqueeze(2).broadcast_to([128, H, 64])
                v.tensor_tensor(z_tok[:, :].rearrange("p (h k) -> p h k", h=H), yv, mb,
                                ALU.subtract)
                v.drain()
                return v.tensor_tensor(z_tok[:, :].rearrange("p (h k) -> p h k", h=H),
                                       z_tok[:, :].rearrange("p (h k) -> p h k", h=H),
                                       rb, ALU.mult)
            nv = S.step("vector", nrm, waits=[("A", max(ga, last_zev))], inc=V, name="gnnrm")
            for cc in range(NDC):
                par = cc % 2
                tt = S.step("tensor", lambda t, cc=cc, par=par: t.transpose(
                    pbf[2 + par][:, 0:128], z_tok[:, 128 * cc:128 * cc + 128], id_s[:, :]),
                    waits=[("V", nv), ("A", tr_bank[par])], inc=T, name="ztr")
                aa = S.step("scalar", lambda s, cc=cc, par=par, mt=mt: s.activation(
                    z3[:, cc, mt * MT:(mt + 1) * MT], pbf[2 + par][:, 0:128], AF.Identity,
                    bias=lnb_c[:, cc:cc + 1], scale=lnw_c[:, cc:cc + 1]),
                    waits=[("T", tt)], inc=A, name="zev")
                tr_bank[par] = aa
                last_zev = aa
        za = S.cur(A)
        gv = S.step("vector", lambda v: v.tensor_mul(z3[:, :, :], z3[:, :, :], g_fm[:, :, :]),
                    waits=[("A", za)], inc=V, name="zg")
        wo_slot = [S.cur(T)] * 4
        wo_bank = [0, 0]
        wo_ev = {}
        lastV = 0
        for do in range(NDC):
            sl = do % 4
            dmi = S.dma("sync", rpj[sl][:, :, :], dbo2[do], DP,
                        waits=[("T", wo_slot[sl])], name="wodma")
            tl = 0
            for cb in range(NDC):
                tl = S.step("tensor", lambda t, sl=sl, cb=cb, do=do: t.matmul(
                    pb[4 + do % 2][:, :], rpj[sl][:, cb, :], z3[:, cb, :],
                    start=(cb == 0), stop=(cb == NDC - 1)),
                    waits=[("DP", dmi), ("V", gv), ("V", wo_bank[do % 2])],
                    inc=T, name="womm")
            wo_slot[sl] = tl
            st["womm"] = tl
            lastV = S.step("vector", lambda v, do=do: v.tensor_tensor(
                resid[:, do, :], resid[:, do, :], pb[4 + do % 2][:, :], ALU.add),
                waits=[("T", tl)], inc=V, name="woev")
            wo_bank[do % 2] = lastV
            wo_ev[do] = lastV
            if do >= 2:
                ln_stats_dc(do - 2, waits=[("V", wo_ev[do - 2])])
        ln_stats_dc(NDC - 2, waits=[("V", wo_ev[NDC - 2])], on_v=True)
        ln_stats_dc(NDC - 1, waits=[("V", wo_ev[NDC - 1])], on_v=True)
        return lastV

    for c in range(nchunk + 1):
        if c > 0:
            xdmi = [0] * NDC
            def post_A(d, out_v, c=c, xdmi=xdmi):
                if c < nchunk:
                    if d >= 2:
                        ln_stats_dc(d - 2, waits=[("G", xdmi[d - 2])])
                    xdmi[d] = S.dma("gpsimd", resid[:, d, :], dx[c][:, d, :], G,
                                    waits=[("V", out_v)] +
                                          ([("G", xdmi[d - 1])] if d >= 1 else []),
                                    name="xdma")
            va = ffn_app("A", post_A)
            if c < nchunk:
                ln_stats_dc(NDC - 2, waits=[("G", xdmi[NDC - 2])], on_v=True)
                ln_stats_dc(NDC - 1, waits=[("G", xdmi[NDC - 1])], on_v=True)
            S.dma("gpsimd", dout[c - 1], stage[:, :, :], G, waits=[("V", va)], name="outdma")
        if c == nchunk:
            break
        if c == 0:
            for dcp in range(NDC):
                ln_stats_dc(dcp)
        mix_ln = do_attn and attn_upto >= 1
        bout = {}
        def post_B(d, out_v, bout=bout):
            bout[d] = out_v
            if d >= 2:
                ln_stats_dc(d - 2, waits=[("V", bout[d - 2])])
        ffn_app("B", post_B if mix_ln else None)
        if mix_ln:
            ln_stats_dc(NDC - 2, waits=[("V", bout[NDC - 2])], on_v=True)
            ln_stats_dc(NDC - 1, waits=[("V", bout[NDC - 1])], on_v=True)
        if do_attn:
            if attn_upto >= 1:
                mix_and_proj()
            if attn_upto >= 2:
                scan(scan_lvl)
            if attn_upto >= 3:
                gn_z_wo()

    S.step("sync", lambda e, v=S.cur(Dm): e.wait_ge(S.sems[Dm], v), name="finwait")
    for snm in (G, D1, D1b, D2, DP):
        S.step("sync", lambda e, snm=snm, v=S.cur(snm): e.wait_ge(S.sems[snm], v),
               name="finwx")
    S.finalize(block)
    es.close()
    return nc


def _bf(x):
    return np.ascontiguousarray(x)


def make_weight_maps(W):
    o = {}
    W1, W2 = W["L0_ffn_W1"], W["L0_ffn_W2"]
    o["w1s"] = np.ascontiguousarray(
        W1.reshape(NFC, 128, NDC, 128).transpose(0, 3, 2, 1).reshape(NFC, 128, NDC * 128))
    o["w2s"] = np.ascontiguousarray(
        W2.reshape(NDC, 128, NFC, 128).transpose(0, 3, 2, 1).reshape(NDC, 128, NFC * 128))
    for nm, ww in [("wks", W["L0_attn_Wk"]), ("wrs", W["L0_attn_Wr"]),
                   ("wgs", W["L0_attn_Wg"]), ("wos", W["L0_attn_Wo"])]:
        o[nm] = np.ascontiguousarray(
            ww.reshape(NDC, 128, NDC, 128).transpose(2, 1, 0, 3).reshape(NDC, 128, NDC * 128))
    o["wvs"] = np.ascontiguousarray(W["L0_attn_Wv"].reshape(NDC, 128, D))
    tm1 = W["L0_attn_tm_w1"]
    o["tm1a"] = np.ascontiguousarray(
        tm1[:, 0:128].reshape(NDC, 128, 128).transpose(1, 0, 2))
    o["tm1b"] = np.ascontiguousarray(
        tm1[:, 128:160].reshape(NDC, 128, 32).transpose(1, 0, 2))
    tm2 = W["L0_attn_tm_w2"]
    a = tm2[0:4].reshape(4, 32, NDC, 128).transpose(0, 1, 2, 3).reshape(128, NDC, 128)
    o["tm2"] = np.ascontiguousarray(a)
    o["tm24"] = np.ascontiguousarray(tm2[4].reshape(32, NDC, 128))
    td1 = W["L0_attn_td_w1"]
    o["td1"] = np.ascontiguousarray(td1.reshape(NDC, 128, 64).transpose(1, 0, 2))
    td2 = W["L0_attn_td_w2"]
    o["td2"] = np.ascontiguousarray(td2.reshape(64, NDC, 128))
    vec = np.zeros((128, 72), np.float32)
    for nm, off in [("x_maa", 0), ("w_maa", 8), ("k_maa", 16), ("v_maa", 24),
                    ("r_maa", 32), ("g_maa", 40)]:
        vec[:, off:off + 8] = W["L0_attn_" + nm].reshape(NDC, 128).T
    vec[:, 48:56] = W["L0_attn_time_decay"].reshape(D)[None].reshape(NDC, 128).T
    vec[:, 56:64] = W["L0_attn_time_first"].reshape(D).reshape(NDC, 128).T
    vec[:, 64] = EPS
    vec[:, 65] = 0.5
    vec[:, 66] = -0.5
    o["vecs"] = vec
    o["b1c"] = np.ascontiguousarray(W["L0_ffn_b1"].reshape(NFC, 128).T)
    o["lnwc"] = np.ascontiguousarray(W["L0_attn_ln_w"].reshape(NDC, 128).T)
    o["lnbc"] = np.ascontiguousarray(W["L0_attn_ln_b"].reshape(NDC, 128).T)
    o["b2r"] = np.ascontiguousarray(W["L0_ffn_b2"].reshape(1, D))
    ml = np.zeros((128, 128), np.uint8)
    for j in range(128):
        ml[j, j + 1:] = 1
    o["mask_lower"] = ml
    o["mask_lower_bf"] = ml.astype(ml_dtypes.bfloat16)
    o["mask_upper"] = (1 - ml).astype(np.uint8)
    o["mask_diag"] = np.eye(128).astype(np.uint8)
    o["ident"] = np.eye(128).astype(ml_dtypes.bfloat16)
    bo = np.zeros((128, 2), ml_dtypes.bfloat16)
    bo[0:64, 0] = 1.0
    bo[64:128, 1] = 1.0
    o["blockones"] = bo
    return o


def shard_x(x, core, nchunk=16):
    T = nchunk * CW
    xc = x[2 * core:2 * core + 2, :T, :]
    a = xc.reshape(2, nchunk, CW, NDC, 128)
    a = a.transpose(1, 4, 3, 0, 2)
    return np.ascontiguousarray(a.reshape(nchunk, 128, NDC, COLS))


def unshard_out(o, nchunk=16):
    a = o.reshape(nchunk, 128, NDC, 2, CW).transpose(3, 0, 4, 2, 1)
    return a.reshape(2, nchunk * CW, D)


def kernel(**inputs):
    W = inputs
    x = np.asarray(inputs["L0_attn_x"], np.float32)
    wm = make_weight_maps({k: np.asarray(v, np.float32) for k, v in inputs.items()
                           if k != "L0_attn_x"})
    nchunk = x.shape[1] // CW
    nc = build(nchunk=nchunk)
    in_maps = []
    for core in range(8):
        m = dict(wm)
        m["x"] = shard_x(x, core, nchunk)
        in_maps.append(m)
    res = run_bass_kernel_spmd(nc, in_maps, core_ids=list(range(8)))
    out = np.empty((16, x.shape[1], D), np.float32)
    for core in range(8):
        out[2 * core:2 * core + 2] = unshard_out(np.asarray(res.results[core]["out"]), nchunk)
    return out


# version 24
# speedup vs baseline: 1.0556x; 1.0556x over previous
import numpy as np
import ml_dtypes
import concourse.bass as bass
import concourse.mybir as mybir
from concourse.bass_utils import run_bass_kernel_spmd
from contextlib import ExitStack

F32 = mybir.dt.float32
BF16 = mybir.dt.bfloat16
FP8 = mybir.dt.float8e4
U8 = mybir.dt.uint8
AF = mybir.ActivationFunctionType
ALU = mybir.AluOpType
AX = mybir.AxisListType

D, FF, NDC, NFC, H, MT = 1024, 4096, 8, 32, 16, 128
CW = 256
COLS = 2 * CW
NMT = COLS // MT
EPS = 1e-5
ENGINES = ("sync", "vector", "scalar", "tensor", "gpsimd")


class Sched:
    def __init__(self, nc, es):
        self.nc, self.es = nc, es
        self.ops = {e: [] for e in ENGINES}
        self.sems, self.counts, self.waited = {}, {}, {}
        self.wmax = {}

    def sem(self, name):
        self.sems[name] = self.es.enter_context(self.nc.semaphore(name))
        self.counts[name] = 0
        return name

    def cur(self, s):
        return self.counts[s]

    def step(self, eng, emit, waits=(), inc=None, inc_by=1, name=None):
        wl = []
        for w in waits:
            if isinstance(w, str):
                w = (w, self.counts[w])
            sname, val = w
            if val <= 0:
                continue
            self.wmax[sname] = max(self.wmax.get(sname, 0), val)
            key = (eng, sname)
            if self.waited.get(key, -1) >= val:
                continue
            self.waited[key] = val
            wl.append((self.sems[sname], val))
        handle, n = (self.sems[inc], inc_by) if inc else (None, 0)
        if inc:
            self.counts[inc] += inc_by

        autodrain = eng in ("vector", "scalar")

        def op(e, wl=wl, emit=emit, handle=handle, n=n, name=name, ad=autodrain):
            for hh, v in wl:
                e.wait_ge(hh, v)
            inst = emit(e)
            if handle is not None:
                assert inst is not None, f"step {name}: inc without instruction"
                inst.then_inc(handle, n)
            if ad:
                e.drain()

        self.ops[eng].append(op)
        return self.counts[inc] if inc else None

    def dma(self, eng, out, in_, sem, waits=(), name=None):
        waits = list(waits) + [(sem, self.wmax.get(sem, 0))]
        return self.step(eng, lambda e, o=out, i=in_: e.dma_start(out=o, in_=i),
                         waits=waits, inc=sem, inc_by=16, name=name)

    def fence(self):
        tgts = [(s, self.counts[s]) for s in self.sems]
        for e in ENGINES:
            wl = []
            for sname, val in tgts:
                if val <= 0:
                    continue
                key = (e, sname)
                if self.waited.get(key, -1) >= val:
                    continue
                self.waited[key] = val
                wl.append((self.sems[sname], val))
            if wl:
                self.ops[e].append(lambda en, wl=wl: [en.wait_ge(hh, v) for hh, v in wl])

    def finalize(self, block):
        for e in ENGINES:
            getattr(block, e)(lambda engine, ops=self.ops[e]: [f(engine) for f in ops])


def build(nchunk=16, do_attn=True, attn_upto=3, scan_lvl=4):
    nc = bass.Bass("TRN2", target_bir_lowering=False, debug=False)

    dx = nc.dram_tensor("x", [nchunk, 128, NDC, COLS], F32, kind="ExternalInput").ap()
    dout = nc.dram_tensor("out", [nchunk, 128, NDC, COLS], F32, kind="ExternalOutput").ap()
    dw1 = nc.dram_tensor("w1s", [NFC, 128, NDC * 128], F32, kind="ExternalInput").ap()
    dw2 = nc.dram_tensor("w2s", [NDC, 128, NFC * 128], F32, kind="ExternalInput").ap()
    dwk = nc.dram_tensor("wks", [NDC, 128, NDC * 128], F32, kind="ExternalInput").ap()
    dwr = nc.dram_tensor("wrs", [NDC, 128, NDC * 128], F32, kind="ExternalInput").ap()
    dwg = nc.dram_tensor("wgs", [NDC, 128, NDC * 128], F32, kind="ExternalInput").ap()
    dwv = nc.dram_tensor("wvs", [NDC, 128, D], F32, kind="ExternalInput").ap()
    dwo = nc.dram_tensor("wos", [NDC, 128, NDC * 128], F32, kind="ExternalInput").ap()
    dtm1a = nc.dram_tensor("tm1a", [128, NDC, 128], F32, kind="ExternalInput").ap()
    dtm1b = nc.dram_tensor("tm1b", [128, NDC, 32], F32, kind="ExternalInput").ap()
    dtm2 = nc.dram_tensor("tm2", [128, NDC, 128], F32, kind="ExternalInput").ap()
    dtm24 = nc.dram_tensor("tm24", [32, NDC, 128], F32, kind="ExternalInput").ap()
    dtd1 = nc.dram_tensor("td1", [128, NDC, 64], F32, kind="ExternalInput").ap()
    dtd2 = nc.dram_tensor("td2", [64, NDC, 128], F32, kind="ExternalInput").ap()
    NV = 72
    dvec = nc.dram_tensor("vecs", [128, NV], F32, kind="ExternalInput").ap()
    db1 = nc.dram_tensor("b1c", [128, NFC], F32, kind="ExternalInput").ap()
    dlnw = nc.dram_tensor("lnwc", [128, NDC], F32, kind="ExternalInput").ap()
    dlnb = nc.dram_tensor("lnbc", [128, NDC], F32, kind="ExternalInput").ap()
    db2r = nc.dram_tensor("b2r", [1, NDC * 128], F32, kind="ExternalInput").ap()
    dml = nc.dram_tensor("mask_lower", [128, 128], U8, kind="ExternalInput").ap()
    dmlb = nc.dram_tensor("mask_lower_bf", [128, 128], BF16, kind="ExternalInput").ap()
    dmu = nc.dram_tensor("mask_upper", [128, 128], U8, kind="ExternalInput").ap()
    dmd = nc.dram_tensor("mask_diag", [128, 128], U8, kind="ExternalInput").ap()
    did = nc.dram_tensor("ident", [128, 128], BF16, kind="ExternalInput").ap()
    dbo = nc.dram_tensor("blockones", [128, 2], BF16, kind="ExternalInput").ap()
    dq1 = nc.dram_tensor("q1", [NFC, 128, NDC * 128], FP8, kind="Internal").ap()
    dq2 = nc.dram_tensor("q2", [NDC, 128, NFC * 128], FP8, kind="Internal").ap()
    dbk = nc.dram_tensor("bk", [NDC, 128, NDC * 128], BF16, kind="Internal").ap()
    dbr = nc.dram_tensor("br", [NDC, 128, NDC * 128], BF16, kind="Internal").ap()
    dbg = nc.dram_tensor("bg", [NDC, 128, NDC * 128], BF16, kind="Internal").ap()
    dbv = nc.dram_tensor("bv", [NDC, 128, D], BF16, kind="Internal").ap()
    dbo2 = nc.dram_tensor("bo2", [NDC, 128, NDC * 128], BF16, kind="Internal").ap()

    es = ExitStack()
    S = Sched(nc, es)
    Dm, G, V, A, T = S.sem("Dm"), S.sem("G"), S.sem("V"), S.sem("A"), S.sem("T")
    D1, D2, DP = S.sem("D1"), S.sem("D2"), S.sem("DP")
    D1b, Ds, D1c = S.sem("D1b"), S.sem("Ds"), S.sem("D1c")
    Dq, Dq2 = S.sem("Dq"), S.sem("Dq2")
    sb = lambda name, shape, dt=F32: es.enter_context(nc.sbuf_tensor(name, shape, dt))

    resid = sb("resid", [128, NDC, COLS])
    S_ma = sb("S_ma", [128, 2, NDC, 64])
    carry = sb("carry", [128, NDC, 2], BF16)
    vecs = sb("vecs_s", [128, NV])
    b1c = sb("b1c_s", [128, NFC])
    lnw_c = sb("lnw_s", [128, NDC])
    lnb_c = sb("lnb_s", [128, NDC])
    b2row = sb("b2r_s", [1, NDC, 128], BF16)
    ml_s = sb("ml_s", [128, 128], U8)
    mlb = sb("mlb_s", [128, 128], BF16)
    mu_s = sb("mu_s", [128, 128], U8)
    md_s = sb("md_s", [128, 128], U8)
    id_s = sb("id_s", [128, 128], BF16)
    bo_s = sb("bo_s", [128, 2], BF16)
    zer_s = sb("zer_s", [128, 128], BF16)
    ones_c = sb("ones_c", [128, 1], BF16)
    ones_fc = sb("ones_fc", [128, 1])
    ones_f = sb("ones_f", [1, 128])
    ones2r = sb("ones2r", [1, 512], BF16)
    tm1a = sb("tm1a_s", [128, NDC, 128], BF16)
    tm1b = sb("tm1b_s", [128, NDC, 32], BF16)
    tm2w = sb("tm2w_s", [128, NDC, 128], BF16)
    tm2w4 = sb("tm2w4_s", [32, NDC, 128], BF16)
    td1w = sb("td1w_s", [128, NDC, 64], BF16)
    td2w = sb("td2w_s", [64, NDC, 128], BF16)
    VMAA = {"x": 0, "w": 8, "k": 16, "v": 24, "r": 32, "g": 40}
    VTD, VU, VEPS, VHALF, VNHALF, VRS1, VRS2 = 48, 56, 64, 65, 66, 67, 68

    xln2 = sb("xln2", [128, NDC, 2, CW + 1], BF16)
    sxy = sb("sxy", [128, NDC * COLS * 4], U8)
    def _sview(off, shape, dt):
        isz = {F32: 4, BF16: 2}[dt]
        n = int(np.prod(shape))
        vv = sxy[:, off:off + n * isz].bitcast(dt)
        pat = "p (" + " ".join(f"a{i}" for i in range(len(shape))) + ") -> p " +               " ".join(f"a{i}" for i in range(len(shape)))
        return vv.rearrange(pat, **{f"a{i}": s for i, s in enumerate(shape)})
    stage = _sview(0, [NDC, COLS], F32)
    sx = _sview(0, [NDC, 2, CW], BF16)
    y_ch = _sview(NDC * COLS * 2, [NMT, D], BF16)
    xnlb = sb("xnlb", [128, NDC * (2 * MT + 1) * 4], U8)
    xn = xnlb[:, 0:NDC * COLS * 2].bitcast(BF16).rearrange(
        "p (a b) -> p a b", a=NDC)
    zx = [sb(f"zx{i}", [128, NDC, COLS], BF16) for i in range(2)]
    bigb = sb("bigb", [128, NFC * COLS * 2], U8)
    def _bview(off, shape):
        n = int(np.prod(shape))
        vv = bigb[:, off:off + n * 2].bitcast(BF16)
        pat = "p (" + " ".join(f"a{i}" for i in range(len(shape))) + ") -> p " +               " ".join(f"a{i}" for i in range(len(shape)))
        return vv.rearrange(pat, **{f"a{i}": s for i, s in enumerate(shape)})
    h_sb = _bview(0, [NFC, COLS])
    r_fm = _bview(0, [NDC, COLS])
    k_fm = _bview(NDC * COLS * 2, [NDC, COLS])
    v_tok = _bview(2 * NDC * COLS * 2, [NMT, D])
    g_fm = _bview(3 * NDC * COLS * 2, [NDC, COLS])
    mixh = sb("mixh", [128, COLS], BF16)
    mixh4 = sb("mixh4", [32, COLS], BF16)
    hid = sb("hid", [64, COLS], BF16)
    tz = [sb(f"tz{i}", [128, COLS]) for i in range(2)]
    z3 = sb("z3", [128, NDC, COLS], BF16)
    xb = z3[:, :, :]
    z_tok = sb("z_tok", [128, D], BF16)
    sq_t = sb("sq_t", [128, D], BF16)
    lwp = xnlb[:, 0:NDC * MT * 4].bitcast(F32).rearrange("p (a b) -> p a b", a=NDC)
    Bx = xnlb[:, NDC * MT * 4:NDC * (2 * MT + 1) * 4].bitcast(F32).rearrange(
        "p (a b) -> p a b", a=NDC)
    e2t = sb("e2t_b", [128, NDC, MT])
    e3t = sb("e3t_b", [128, NDC, MT])
    RtX = [sb(f"Rt_{i}", [128, NDC, MT], BF16) for i in range(2)]
    KtX = [sb(f"Kt_{i}", [128, NDC, MT], BF16) for i in range(2)]
    KhFX = [sb(f"KhF_{i}", [128, NDC, MT], BF16) for i in range(2)]
    mrkuX = [sb(f"mrku_{i}", [128, NDC, MT], BF16) for i in range(2)]
    SbfX = [sb(f"Sbf_{i}", [128, NDC, 64], BF16) for i in range(2)]
    Khtok = [sb(f"Khtok{i}", [128, 128], BF16) for i in range(2)]
    Phat = [[sb(f"Phat{i}{p}", [128, 128], BF16) for p in range(2)] for i in range(2)]
    negcX = [sb(f"negc_{i}", [128, 4, NDC]) for i in range(2)]
    negc4 = sb("negc4", [128, NDC])
    negc5X = [sb(f"negc5_{i}", [128, NDC]) for i in range(2)]
    gn_m = sb("gn_m", [128, NMT, H])
    gn_v = sb("gn_v", [128, NMT, H])
    gn_r = sb("gn_r", [128, NMT, H])
    s_m = sb("s_m", [1, 512])
    s_r = sb("s_r", [1, 512])
    xcp2 = sb("xcp2", [128, 2, COLS], BF16)
    sq2 = sb("sq2", [128, 2, COLS], BF16)
    ones_l = sb("ones_l", [128, MT])
    qs = z_tok[:, :]
    qa = sq_t[:, :]
    q8buf = [mixh[:, :].bitcast(FP8), tz[0][:, 0:256].bitcast(FP8)]
    negw = z3[:, :, :].rearrange("p a b -> p (a b)").bitcast(F32)[:, 0:1024]
    wpiece = [zx[i][:, 0:4, :].rearrange("p a b -> p (a b)").bitcast(F32)
              for i in range(2)]
    rw1t = sb("rw1t", [128, 6, NDC, 128], FP8)
    rw1 = [rw1t[:, i, :, :] for i in range(6)]
    rw2 = [sb(f"rw2_{i}", [128, NFC, 128], FP8) for i in range(3)]
    rpj = [sb(f"rpj_{i}", [128, NDC, 128], BF16) for i in range(4)]
    rwv = [sb(f"rwv_{i}", [128, D], BF16) for i in range(2)]

    import os as _os
    if _os.environ.get("SBUF_DBG"):
        from concourse.bass_primitives import MemorySpace as _MS
        print("SBUF allocated:", nc.bytes_allocated(_MS.SBUF))
    pb = [es.enter_context(nc.psum_tensor(f"pb{i}", [128, 512], F32)) for i in range(8)]
    pbf = [p[:, :].bitcast(BF16) for p in pb]

    block = es.enter_context(nc.Block())

    for dst, src in [(vecs[:, :], dvec), (b1c[:, :], db1), (lnw_c[:, :], dlnw),
                     (lnb_c[:, :], dlnb), (ml_s[:, :], dml), (mlb[:, :], dmlb),
                     (mu_s[:, :], dmu), (md_s[:, :], dmd),
                     (id_s[:, :], did), (bo_s[:, :], dbo)]:
        S.dma("sync", dst, src, Dm)
    CON_D = S.cur(Dm)
    S.dma("gpsimd", b2row[:, :, :].rearrange("p a b -> p (a b)"), db2r, G)
    for dst, src in [(dbk, dwk), (dbr, dwr), (dbg, dwg), (dbv, dwv), (dbo2, dwo)]:
        S.dma("gpsimd", dst, src, G)
    for dst, src in [(tm1a[:, :, :], dtm1a), (tm1b[:, :, :], dtm1b),
                     (tm2w[:, :, :], dtm2), (tm2w4[:, :, :], dtm24),
                     (td1w[:, :, :], dtd1), (td2w[:, :, :], dtd2)]:
        S.dma("gpsimd", dst, src, G)
    SMALLW_G = S.cur(G)

    def init_consts(v):
        v.memset(S_ma[:, :, :, :], 0.0)
        for i in range(2):
            for p in range(2):
                v.memset(Phat[i][p][:, :], 0.0)
        v.memset(carry[:, :, :], 0.0)
        v.memset(zer_s[:, :], 0.0)
        v.memset(ones_c[:, :], 1.0)
        v.memset(ones_fc[:, :], 1.0)
        v.memset(ones_f[:, :], 1.0)
        v.memset(ones2r[:, :], 2.0)
        v.memset(Bx[:, :, 0:1], 0.0)
        return v.memset(ones_l[:, :], 1.0)
    S.step("vector", init_consts, inc=V, name="init")
    INIT_V = S.cur(V)

    tq_state = {"st_q": [None, None]}

    def ternarize(dw, dq, nslice, inner, rs_off):
        npp = inner // 1024
        pieces = [(i, j) for i in range(nslice) for j in range(npp)]
        abs_m = [max(S.cur(V), INIT_V)] * 2
        mm_piece = [S.cur(T), S.cur(T)]
        mm_m = 0
        for pi, (i, j) in enumerate(pieces):
            w = wpiece[pi % 2]
            qe, qsem = ("sync", Dm) if pi % 2 == 0 else ("scalar", Ds)
            dmi = S.dma(qe, w[:, :], dw[i][:, j * 1024:(j + 1) * 1024], qsem,
                        waits=[("T", mm_piece[pi % 2]), ("V", abs_m[pi % 2])], name="tw_ld")
            def vabs(v, w=w):
                v.tensor_scalar_mul(negw[:, :], w[:, :], -1.0)
                v.drain()
                return v.tensor_tensor(w[:, :], w[:, :], negw[:, :], ALU.max)
            am = S.step("vector", vabs,
                        waits=[(qsem, dmi), ("T", mm_piece[pi % 2])],
                        inc=V, name="tw_abs")
            abs_m[pi % 2] = am
            for hh in range(2):
                mm_m = S.step("tensor", lambda t, w=w, hh=hh, st=(pi == 0 and hh == 0),
                              sp=(pi == len(pieces) - 1 and hh == 1): t.matmul(
                    pb[6][0:1, :], ones_fc[:, :], w[:, hh * 512:(hh + 1) * 512],
                    start=st, stop=sp), waits=[("V", am)],
                    inc=T, name="tw_sum")
            mm_piece[pi % 2] = mm_m
        def fin(vv, n=float(nslice * inner * 128)):
            vv.tensor_reduce(s_m[0:1, 0:8], pb[6][0:1, :].rearrange("p (a b) -> p a b", a=8),
                             AX.X, ALU.add)
            vv.drain()
            vv.tensor_reduce(s_m[0:1, 8:9], s_m[0:1, 0:8], AX.X, ALU.add)
            vv.drain()
            vv.tensor_scalar(s_m[0:1, 9:10], s_m[0:1, 8:9], 1.0 / n, 1e-8, ALU.mult, ALU.max)
            vv.drain()
            return vv.tensor_scalar_mul(s_m[0:1, 10:11], s_m[0:1, 9:10], -0.5)
        S.step("vector", fin, waits=[("T", mm_m)], inc=V, name="tw_fin")
        S.step("tensor", lambda t: t.matmul(pb[7][:, 0:1], ones_f[:, :], s_m[0:1, 10:11],
                                            start=True, stop=True),
               waits=[("V", S.cur(V))], inc=T, name="tw_bc")
        S.step("vector", lambda v, o=rs_off: v.tensor_copy(vecs[:, o:o + 1], pb[7][:, 0:1]),
               waits=[("T", S.cur(T))], inc=V, name="tw_rs")
        RS = S.cur(V)
        sg_m = [S.cur(A)] * 2
        last_vm = S.cur(V)
        t_p1 = S.cur(T)
        st_q = tq_state["st_q"]
        for pi, (i, j) in enumerate(pieces):
            w = wpiece[pi % 2]
            qe, qsem = ("sync", Dm) if pi % 2 == 0 else ("scalar", Ds)
            dmi = S.dma(qe, w[:, :], dw[i][:, j * 1024:(j + 1) * 1024], qsem,
                        waits=[("A", sg_m[pi % 2]), ("T", t_p1)], name="tq_ld")
            def sgf(s, w=w, o=rs_off):
                s.activation(qa[:, :], w[:, :], AF.Sign, bias=vecs[:, o:o + 1], scale=1.0)
                return s.activation(qs[:, :], w[:, :], AF.Sign, bias=vecs[:, o:o + 1],
                                    scale=-1.0)
            am = S.step("scalar", sgf, waits=[(qsem, dmi), ("V", max(RS, last_vm))],
                        inc=A, name="tq_sign")
            sg_m[pi % 2] = am
            qb = q8buf[pi % 2]
            qsem2 = Dq if pi % 2 == 0 else Dq2
            vm = S.step("vector", lambda v, qb=qb: v.tensor_sub(qb, qa[:, :], qs[:, :]),
                        waits=[("A", am)] +
                              ([st_q[pi % 2]] if st_q[pi % 2] else []),
                        inc=V, name="tq_add")
            last_vm = vm
            smi = S.dma(qe, dq[i][:, j * 1024:(j + 1) * 1024], qb, qsem2,
                        waits=[("V", vm)], name="tq_st")
            st_q[pi % 2] = (qsem2, smi)
    ternarize(dw1, dq1, NFC, NDC * 128, VRS1)
    ternarize(dw2, dq2, NDC, NFC * 128, VRS2)
    S.dma("sync", resid[:, :, :], dx[0], Dm, name="x0")
    S.fence()

    st = {"womm": 0, "ln_pb6": S.cur(A), "ln_pb7": S.cur(V),
          "sq2": [0, 0], "ln_qmm": 0, "kha": [0, 0]}

    def ln_stats_dc(dc, waits=(), on_v=False):
        sl = dc % 2
        if on_v:
            def cpsq_v(v, dc=dc, sl=sl):
                v.tensor_copy(xcp2[:, sl, :], resid[:, dc, :])
                v.drain()
                return v.tensor_mul(sq2[:, sl, :], resid[:, dc, :], resid[:, dc, :])
            am = S.step("vector", cpsq_v,
                        waits=list(waits) + [("T", st["sq2"][sl])], inc=V,
                        name="ln_cpsq_v")
            w = [("V", am)]
        else:
            def cpsq(s, dc=dc, sl=sl):
                s.activation(xcp2[:, sl, :], resid[:, dc, :], AF.Copy)
                return s.activation(sq2[:, sl, :], resid[:, dc, :], AF.Square)
            am = S.step("scalar", cpsq,
                        waits=list(waits) + [("T", st["sq2"][sl])], inc=A, name="ln_cpsq")
            w = [("A", am)]
        if dc == 0:
            w += [("V", st["ln_pb7"])]
        S.step("tensor", lambda t, dc=dc, sl=sl: t.matmul(
            pb[6][0:1, :], ones_c[:, :], xcp2[:, sl, :],
            start=(dc == 0), stop=(dc == NDC - 1)),
            waits=w, inc=T, name="ln_smm")
        t2 = S.step("tensor", lambda t, dc=dc, sl=sl: t.matmul(
            pb[7][0:1, :], ones_c[:, :], sq2[:, sl, :],
            start=(dc == 0), stop=(dc == NDC - 1)),
            waits=[], inc=T, name="ln_qmm")
        st["sq2"][sl] = t2
        st["ln_qmm"] = t2

    def ln_finish(dst_kind):
        def lnvar(v):
            v.tensor_scalar_mul(s_m[0:1, :], pb[6][0:1, :], 1.0 / D)
            v.drain()
            v.tensor_mul(s_r[0:1, :], s_m[0:1, :], s_m[0:1, :])
            v.drain()
            return v.scalar_tensor_tensor(
                s_r[0:1, :], pb[7][0:1, :], 1.0 / D, s_r[0:1, :], ALU.mult, ALU.subtract)
        vv = S.step("vector", lnvar, waits=[("T", st["ln_qmm"])], inc=V, name="ln_var")
        st["ln_pb6"] = st["ln_pb7"] = vv
        def stats2(s):
            s.activation(s_r[0:1, :], s_r[0:1, :], AF.Ln, bias=vecs[0:1, VEPS:VEPS + 1])
            s.drain()
            return s.activation(s_r[0:1, :], s_r[0:1, :], AF.Exp, scale=-0.5)
        ca = S.step("scalar", stats2, waits=[("V", vv)], inc=A, name="ln_rstd")
        S.step("tensor", lambda t: t.matmul(pb[2][:, :], ones_f[:, :], s_m[0:1, :],
                                            start=True, stop=True),
               waits=[("A", ca)], inc=T, name="ln_bm")
        tb = S.step("tensor", lambda t: t.matmul(pb[3][:, :], ones_f[:, :], s_r[0:1, :],
                                                 start=True, stop=True),
                    waits=[], inc=T, name="ln_ba")
        lnv_dc = []
        for h in range(4):
            ds = slice(2 * h, 2 * h + 2)
            S.step("vector", lambda v, ds=ds: v.tensor_tensor(
                xb[:, ds, :], resid[:, ds, :],
                pb[2][:, :].unsqueeze(1).broadcast_to([128, 2, 512]),
                ALU.subtract),
                waits=[("T", tb), ("T", st["womm"])], inc=V, name="ln_sub")
            if dst_kind == "xn":
                vm = S.step("vector", lambda v, ds=ds: v.tensor_tensor(
                    xn[:, ds, :], xb[:, ds, :],
                    pb[3][:, :].unsqueeze(1).broadcast_to([128, 2, 512]),
                    ALU.mult), waits=[], inc=V, name="ln_mul")
            else:
                vm = S.step("vector", lambda v, ds=ds: v.tensor_tensor(
                    xln2[:, ds, :, 1:], xb[:, ds, :].rearrange("p a (b c) -> p a b c", b=2),
                    pb[3][:, :].rearrange("p (b c) -> p b c", b=2).unsqueeze(1)
                    .broadcast_to([128, 2, 2, CW]),
                    ALU.mult), waits=[], inc=V, name="ln_mul")
            lnv_dc += [vm, vm]
        return lnv_dc

    def ffn_app(app, post_d_cb=None):
        lnv_dc = ln_finish("xn")
        gel_bank = [0, 0]
        slot_cons = [S.cur(T)] * 6
        dmis = {}

        def w1dma(f0):
            sl = f0 % 6
            sem = (D1, D1b, D1c)[(f0 // 2) % 3]
            gate = max(slot_cons[sl], slot_cons[sl + 1])
            dmi = S.dma("sync", rw1t[:, sl:sl + 2, :, :],
                        dq1[f0:f0 + 2].transpose([1, 0, 2]), sem,
                        waits=[("A", gate) if f0 >= 6 else ("T", gate)],
                        name="w1dma")
            dmis[f0] = dmis[f0 + 1] = (sem, dmi)
        w1dma(0)
        w1dma(2)
        w1dma(4)
        for f in range(NFC):
            sl = f % 6
            tl = 0
            for dc in range(NDC):
                tl = S.step("tensor", lambda t, sl=sl, dc=dc, f=f: t.matmul(
                    pb[f % 2][:, :], rw1[sl][:, dc, :], xn[:, dc, :],
                    start=(dc == 0), stop=(dc == NDC - 1)),
                    waits=[dmis[f], ("V", lnv_dc[dc]), ("A", gel_bank[f % 2])],
                    inc=T, name="mm1")
            am = S.step("scalar", lambda s, f=f: s.activation(
                h_sb[:, f, :], pb[f % 2][:, :], AF.Gelu, bias=b1c[:, f:f + 1], scale=0.5),
                waits=[("T", tl)], inc=A, name="gelu")
            gel_bank[f % 2] = am
            slot_cons[sl] = am
            if f % 2 == 1 and f + 5 < NFC:
                w1dma(f + 5)
        HD = S.cur(A)
        ev_bank = [0, 0]
        w2_cons = [S.cur(T)] * 3
        out_v = 0
        for d in range(NDC):
            sl = d % 3
            dmi = S.dma("sync", rw2[sl][:, :, :], dq2[d], D2,
                        waits=[("T", w2_cons[sl])], name="w2dma")
            tl = 0
            for f in range(NFC):
                tl = S.step("tensor", lambda t, sl=sl, f=f, d=d: t.matmul(
                    pb[4 + d % 2][:, :], rw2[sl][:, f, :], h_sb[:, f, :],
                    start=(f == 0), stop=False),
                    waits=[("D2", dmi), ("A", HD), ("V", ev_bank[d % 2])], inc=T, name="mm2")
            tl = S.step("tensor", lambda t, d=d: t.matmul(
                pb[4 + d % 2][:, :], b2row[0:1, d, :], ones2r[0:1, :],
                start=False, stop=True), waits=[], inc=T, name="b2mm")
            w2_cons[sl] = tl
            dst = stage if app == "A" else resid
            out_v = S.step("vector", lambda v, d=d, dst=dst: v.scalar_tensor_tensor(
                dst[:, d, :], pb[4 + d % 2][:, :], 0.25, resid[:, d, :],
                ALU.mult, ALU.add), waits=[("T", tl)], inc=V, name="ffn_out")
            ev_bank[d % 2] = out_v
            if post_d_cb is not None:
                post_d_cb(d, out_v)
        return S.cur(V)

    def mix_and_proj():
        S.step("vector", lambda v: v.tensor_copy(
            xln2[:, :, :, 0:1].squeeze(3), carry[:, :, :]),
            waits=["Dm", "G"], inc=V, name="halo")
        ln_finish("xln2")
        pxw = S.cur(T)
        px_v = []
        for h in range(4):
            ds = slice(2 * h, 2 * h + 2)
            S.step("vector", lambda v, ds=ds: v.tensor_tensor(
                sx[:, ds, :, :], xln2[:, ds, :, 0:CW], xln2[:, ds, :, 1:CW + 1],
                ALU.subtract), waits=[], inc=V, name="sx")
            for dc in (2 * h, 2 * h + 1):
                pv = S.step("vector", lambda v, dc=dc: v.scalar_tensor_tensor(
                    zx[0][:, dc, :].rearrange("p (b c) -> p b c", b=2), sx[:, dc, :, :],
                    vecs[:, VMAA["x"] + dc:VMAA["x"] + dc + 1],
                    xln2[:, dc, :, 1:],
                    ALU.mult, ALU.add), waits=[("T", pxw)], inc=V, name="px")
                px_v.append(pv)
        S.step("vector", lambda v: v.tensor_copy(
            carry[:, :, :], xln2[:, :, :, CW:CW + 1].squeeze(3)),
            waits=[], inc=V, name="carry")
        pxv = S.cur(V)
        tl = tlb = 0
        for dc in range(NDC):
            tl = S.step("tensor", lambda t, dc=dc: t.matmul(
                pb[0][:, :], tm1a[:, dc, :], zx[0][:, dc, :],
                start=(dc == 0), stop=(dc == NDC - 1)),
                waits=[("V", px_v[dc]), ("G", SMALLW_G)], inc=T, name="tm1a")
            tlb = S.step("tensor", lambda t, dc=dc: t.matmul(
                pb[1][0:32, :], tm1b[:, dc, :], zx[0][:, dc, :],
                start=(dc == 0), stop=(dc == NDC - 1)),
                waits=[], inc=T, name="tm1b")
        S.step("scalar", lambda s: s.activation(mixh[:, :], pb[0][:, :], AF.Tanh),
               waits=[("T", tlb)], inc=A, name="tmtanh")
        mha = S.step("scalar", lambda s: s.activation(mixh4[:, :], pb[1][0:32, :], AF.Tanh),
                     waits=[], inc=A, name="tmtanh4")

        def build_zx(z, zi, slot, prev_use_T, prev_vm=0):
            vm = 0
            a_gate = max(mha, S.cur(A))
            bank_ev = [prev_vm] * 4
            vm_percb = []
            tmm = {}

            def emit_mz(cb):
                bank = cb % 4
                if zi < 4:
                    lh, rh, tp = tm2w[32 * zi:32 * zi + 32, cb, :], \
                        mixh[32 * zi:32 * zi + 32, :], (32 * zi, 0)
                else:
                    lh, rh, tp = tm2w4[:, cb, :], mixh4[:, :], (0, 0)
                tmm[cb] = S.step("tensor", lambda t, lh=lh, rh=rh, tp=tp, bank=bank: t.matmul(
                    pb[bank][:, :], lh, rh, start=True, stop=True, tile_position=tp),
                    waits=[("A", a_gate), ("V", bank_ev[bank]),
                           ("T", prev_use_T if cb == 0 else 0)],
                    inc=T, name="mz")

            for cb in range(4):
                emit_mz(cb)
            for cb in range(NDC):
                bank = cb % 4
                def mzzx(v, cb=cb, z=z, bank=bank, slot=slot):
                    v.scalar_tensor_tensor(
                        tz[cb % 2][:, :].rearrange("p (b c) -> p b c", b=2),
                        pb[bank][:, :].rearrange("p (b c) -> p b c", b=2),
                        vecs[:, VMAA[z] + cb:VMAA[z] + cb + 1],
                        sx[:, cb, :, :], ALU.add, ALU.mult)
                    v.drain()
                    return v.tensor_tensor(
                        zx[slot][:, cb, :].rearrange("p (b c) -> p b c", b=2),
                        tz[cb % 2][:, :].rearrange("p (b c) -> p b c", b=2),
                        xln2[:, cb, :, 1:], ALU.add)
                vm = S.step("vector", mzzx, waits=[("T", tmm[cb])], inc=V, name="mzzx")
                bank_ev[bank] = vm
                vm_percb.append(vm)
                if cb + 4 < NDC:
                    emit_mz(cb + 4)
            return vm, vm_percb

        zx_T = {0: tlb, 1: 0}
        slot = 1
        proj_slot = [0] * 4
        proj_slot_t = S.cur(T)
        pj_bank = [0, 0]
        prev_vm = 0
        for z, zi in [("w", 0), ("k", 1), ("v", 2), ("r", 3), ("g", 4)]:
            vm, vm_percb = build_zx(z, zi, slot, zx_T[slot], prev_vm)
            prev_vm = vm
            if z == "w":
                tl2 = 0
                for dc in range(NDC):
                    tl2 = S.step("tensor", lambda t, dc=dc, slot=slot: t.matmul(
                        pb[6][0:64, :], td1w[:, dc, :], zx[slot][:, dc, :],
                        start=(dc == 0), stop=(dc == NDC - 1)),
                        waits=[("V", vm_percb[dc])], inc=T, name="td1")
                S.step("scalar", lambda s: s.activation(hid[:, :], pb[6][0:64, :], AF.Tanh),
                       waits=[("T", tl2)], inc=A, name="tdtanh")
                zx_T[slot] = tl2
            elif z == "v":
                lastT = 0
                rwv_cons = [0, 0]
                for dcq in range(NDC):
                    dmi = S.dma("sync", rwv[dcq % 2][:, :], dbv[dcq], DP,
                                waits=[("T", rwv_cons[dcq % 2])], name="wvdma")
                    for mtv in range(NMT):
                        for half in range(2):
                            lastT = S.step("tensor",
                                lambda t, dcq=dcq, mtv=mtv, half=half, slot=slot: t.matmul(
                                    pb[2 * mtv + half][:, :],
                                    zx[slot][:, dcq, mtv * MT:(mtv + 1) * MT],
                                    rwv[dcq % 2][:, half * 512:(half + 1) * 512],
                                    start=(dcq == 0), stop=(dcq == NDC - 1)),
                                waits=[("DP", dmi),
                                       ("V", vm if dcq == 0 else vm_percb[dcq]),
                                       ("A", S.cur(A) if dcq == 0 else 0)],
                                inc=T, name="vproj")
                    rwv_cons[dcq % 2] = lastT
                for mtv in range(NMT):
                    for half in range(2):
                        S.step("scalar", lambda s, mtv=mtv, half=half: s.activation(
                            v_tok[:, mtv, half * 512:(half + 1) * 512],
                            pb[2 * mtv + half][:, :], AF.Copy),
                            waits=[("T", lastT)], inc=A, name="vev")
                zx_T[slot] = lastT
                pj_bank = [S.cur(A)] * 2
            else:
                dst = {"k": k_fm, "r": r_fm, "g": g_fm}[z]
                fn = AF.Silu if z == "g" else AF.Copy
                dsrc = {"k": dbk, "r": dbr, "g": dbg}[z]
                lastT = 0
                for rc in range(NDC):
                    sl = rc % 4
                    dmi = S.dma("sync", rpj[sl][:, :, :], dsrc[rc], DP,
                                waits=[("A", proj_slot[sl]), ("T", proj_slot_t)],
                                name="pjdma")
                    for dc in range(NDC):
                        lastT = S.step("tensor", lambda t, sl=sl, dc=dc, rc=rc, slot=slot:
                            t.matmul(pb[4 + rc % 2][:, :], rpj[sl][:, dc, :], zx[slot][:, dc, :],
                                     start=(dc == 0), stop=(dc == NDC - 1)),
                            waits=[("DP", dmi), ("V", vm_percb[dc]), ("A", pj_bank[rc % 2])],
                            inc=T, name="proj")
                    am = S.step("scalar", lambda s, rc=rc, dst=dst, fn=fn: s.activation(
                        dst[:, rc, :], pb[4 + rc % 2][:, :], fn), waits=[("T", lastT)],
                        inc=A, name="projev")
                    proj_slot[sl] = am
                    pj_bank[rc % 2] = am
                zx_T[slot] = lastT
            slot = 1 - slot
        return S.cur(A)

    def scan(scan_lvl=4):
        dmc = S.cur(Dm)
        dmg = S.cur(G)
        a0 = S.cur(A)
        par_sel = [0, 0]
        par_kv = [0, 0]
        last_ymm = S.cur(T)
        st7 = {"evac": [a0, a0], "lwp_cons": 0, "yev6": a0, "yev7": a0,
               "ymm_pmt": last_ymm}
        chain = {}

        def emit_td2_wave(mt, wave):
            bank = pb[wave]
            t_last = 0
            for i in range(4):
                cc = 4 * wave + i
                t_last = S.step("tensor", lambda t, cc=cc, i=i, mt=mt, bank=bank: t.matmul(
                    bank[:, 128 * i:128 * i + 128], td2w[:, cc, :],
                    hid[:, mt * MT:mt * MT + MT], start=True, stop=True),
                    waits=[("A", st7["evac"][wave]), ("A", a0 if mt == 0 else 0)],
                    inc=T, name="td2")
            def lwpev(s, wave=wave, bank=bank):
                last = None
                for i in range(4):
                    cc = 4 * wave + i
                    last = s.activation(lwp[:, cc, :], bank[:, 128 * i:128 * i + 128],
                                        AF.Exp, bias=vecs[:, VTD + cc:VTD + cc + 1])
                return last
            st7["evac"][wave] = S.step("scalar", lwpev,
                                 waits=[("T", t_last), ("V", st7["lwp_cons"])],
                                 inc=A, name="lwpev")

        def emit_cum(mt):
            def cum(v):
                v.memset(Bx[:, :, 0:1], 0.0)
                for cc in range(NDC):
                    v.tensor_tensor_scan(Bx[:, cc, 1:MT + 1], ones_l[:, :], lwp[:, cc, :],
                                         0.0, ALU.mult, ALU.add)
                v.drain()
                negc = negcX[mt % 2]
                v.tensor_copy(negc[:, 0, :], Bx[:, :, MT // 2:MT // 2 + 1].squeeze(2))
                v.tensor_copy(negc[:, 1, :], Bx[:, :, MT:MT + 1].squeeze(2))
                v.drain()
                v.tensor_tensor(negc4[:, :], negc[:, 1, :], negc[:, 0, :], ALU.subtract)
                v.drain()
                return v.tensor_tensor(Bx[:, :, :], Bx[:, :, :],
                                       negc[:, 0, :].unsqueeze(2)
                                       .broadcast_to([128, NDC, MT + 1]), ALU.subtract)
            return S.step("vector", cum, waits=[("A", max(st7["evac"]))], inc=V,
                          name="cumsum")

        def emit_exps(mt, cv):
            def exps(s):
                negc = negcX[mt % 2]
                s.activation(lwp[:, :, :], Bx[:, :, 0:MT], AF.Exp, scale=-1.0)
                s.activation(e2t[:, :, :], Bx[:, :, 1:MT + 1], AF.Exp)
                s.activation(negc[:, 2, :], negc[:, 0, :], AF.Exp, scale=-1.0)
                s.activation(negc5X[mt % 2][:, :], negc4[:, :], AF.Exp, scale=-1.0)
                return s.activation(negc[:, 3, :], negc[:, 1, :], AF.Exp, scale=-1.0)
            return S.step("scalar", exps, waits=[("V", cv)], inc=A, name="exps")

        def emit_scal(mt, ea):
            alt = mt % 2
            def scal(v, mt=mt, alt=alt):
                m0 = mt * MT
                rsl = r_fm[:, :, m0:m0 + MT]
                ksl = k_fm[:, :, m0:m0 + MT]
                v.tensor_tensor(e3t[:, :, :], e2t[:, :, :],
                                negc5X[alt][:, :].unsqueeze(2).broadcast_to([128, NDC, MT]),
                                ALU.mult)
                v.tensor_tensor(RtX[alt][:, :, :], rsl, lwp[:, :, :], ALU.mult)
                v.tensor_tensor(KtX[alt][:, :, :], ksl, e2t[:, :, :], ALU.mult)
                v.tensor_tensor(mrkuX[alt][:, :, :], rsl, ksl, ALU.mult)
                v.drain()
                v.tensor_tensor(KhFX[alt][:, :, :], ksl, e3t[:, :, :], ALU.mult)
                return v.tensor_tensor(mrkuX[alt][:, :, :], mrkuX[alt][:, :, :],
                                       vecs[:, VU:VU + 8].unsqueeze(2)
                                       .broadcast_to([128, NDC, MT]), ALU.mult)
            sv = S.step("vector", scal, waits=[("A", ea), ("T", st7["ymm_pmt"])],
                        inc=V, name="scal")
            st7["lwp_cons"] = sv
            return sv

        def emit_sbf(mt):
            alt = mt % 2
            b = mt // (NMT // 2)
            return S.step("vector", lambda v, alt=alt, b=b: v.tensor_tensor(
                SbfX[alt][:, :, :], S_ma[:, b, :, :],
                negcX[alt][:, 2, :].unsqueeze(2).broadcast_to([128, NDC, 64]), ALU.mult),
                waits=[("T", st7["ymm_pmt"])], inc=V, name="sbf")

        emit_td2_wave(0, 0)
        emit_td2_wave(0, 1)
        cv = emit_cum(0)
        ea = emit_exps(0, cv)
        chain[0] = (emit_scal(0, ea), 0)
        chain[0] = (chain[0][0], emit_sbf(0))

        for mt in range(NMT):
            alt = mt % 2
            b = mt // (NMT // 2)
            Rt, Kt, KhF, mrku, Sbf = RtX[alt], KtX[alt], KhFX[alt], mrkuX[alt], SbfX[alt]
            negc = negcX[alt]
            sv_m, sbf_m = chain[mt]
            lastY = 0
            tp_done = {}
            st7["ymm_pmt"] = last_ymm

            def emit_pmm(cc, Rt=Rt, Kt=Kt, KhF=KhF, mrku=mrku, sv_m=sv_m):
                par = cc % 2
                PB, PK, PKb = pb[2 + par], pb[4 + par], pbf[4 + par]
                def pmm2(t, cc=cc, PB=PB, PK=PK, PKb=PKb):
                    t.matmul(PB[:, 128:256], Kt[0:64, cc, :], Rt[0:64, cc, :],
                             start=True, stop=True)
                    t.matmul(PK[:, 0:128], Kt[64:128, cc, :], Rt[64:128, cc, :],
                             start=True, stop=True)
                    t.matmul(PB[:, 384:386], mrku[:, cc, :], bo_s[:, :],
                             start=True, stop=True)
                    return t.transpose(PKb[:, 768:896], KhF[:, cc, :], id_s[:, :])
                tp_done[cc] = S.step("tensor", pmm2,
                                     waits=[("V", sv_m), ("V", par_kv[par]),
                                            ("V", par_sel[par])],
                                     inc=T, name="pmm")

            emit_pmm(0)
            pend_supd = None
            for cc in range(NDC):
                par = cc % 2
                PB, PK, PKb = pb[2 + par], pb[4 + par], pbf[4 + par]
                tp = tp_done[cc]
                def sel(v, par=par, PB=PB, PK=PK, PKb=PKb):
                    for p in range(2):
                        psrc = PB[:, 128:256] if p == 0 else PK[:, 0:128]
                        v.copy_predicated(Phat[par][p][:, :], ml_s[:, :], psrc)
                        v.drain()
                        v.copy_predicated(Phat[par][p][:, :], md_s[:, :],
                                          PB[:, 384 + p:385 + p].broadcast_to([128, 128]))
                        v.drain()
                    return v.tensor_copy(Khtok[par][:, :], PKb[:, 768:896])
                sm = S.step("vector", sel, waits=[("T", tp)], inc=V, name="sel")
                par_sel[par] = sm
                if pend_supd is not None:
                    pend_supd()
                if cc + 1 < NDC:
                    emit_pmm(cc + 1)
                def ymm(t, cc=cc, par=par, PK=PK, mt=mt, Rt=Rt, Sbf=Sbf):
                    for p in range(2):
                        hh = 2 * cc + p
                        ysl = pb[6 + hh // 8][:, 64 * (hh % 8):64 * (hh % 8) + 64]
                        vsl = v_tok[:, mt, 64 * hh:64 * hh + 64]
                        t.matmul(ysl, Phat[par][p][:, :], vsl, start=True, stop=False)
                        t.matmul(ysl, Rt[64 * p:64 * p + 64, cc, :],
                                 Sbf[64 * p:64 * p + 64, cc, :], start=False, stop=True)
                    last = None
                    for p in range(2):
                        vsl = v_tok[:, mt, 64 * (2 * cc + p):64 * (2 * cc + p) + 64]
                        last = t.matmul(PK[64 * p:64 * p + 64, 448:512],
                                        Khtok[par][:, 64 * p:64 * p + 64], vsl,
                                        start=True, stop=True, tile_position=(0, 64 * p))
                    return last
                ty = S.step("tensor", ymm,
                            waits=[("V", sm), ("V", sbf_m),
                                   ("A", st7["yev6"] if cc == 0 else
                                    (st7["yev7"] if cc == 4 else 0))],
                            inc=T, name="ymm")
                lastY = ty
                last_ymm = ty
                def mk_supd(par=par, PK=PK, b=b, cc=cc, negc=negc, ty=ty):
                    def go():
                        kvv = S.step("vector",
                            lambda v: v.scalar_tensor_tensor(
                                S_ma[:, b, cc, :], S_ma[:, b, cc, :],
                                negc[:, 3, cc:cc + 1],
                                PK[:, 448:512], ALU.mult, ALU.add),
                            waits=[("T", ty)], inc=V, name="supd")
                        par_kv[par] = kvv
                    return go
                pend_supd = mk_supd()
                if cc == 3:
                    st7["yev6"] = S.step("scalar", lambda s, mt=mt: s.activation(
                        y_ch[:, mt, 0:512], pb[6][:, :], AF.Copy, scale=0.125),
                        waits=[("T", ty), ("Dm", dmc), ("G", dmg)], inc=A, name="yev6")
                if mt + 1 < NMT:
                    if cc == 1:
                        emit_td2_wave(mt + 1, 0)
                    elif cc == 3:
                        emit_td2_wave(mt + 1, 1)
                    elif cc == 4:
                        nxt_cv = emit_cum(mt + 1)
                    elif cc == 5:
                        nxt_ea = emit_exps(mt + 1, nxt_cv)
                    elif cc == 6:
                        nxt_sv = emit_scal(mt + 1, nxt_ea)
            pend_supd()
            if mt + 1 < NMT:
                chain[mt + 1] = (nxt_sv, emit_sbf(mt + 1))
            st7["yev7"] = S.step("scalar", lambda s, mt=mt: s.activation(
                y_ch[:, mt, 512:1024], pb[7][:, :], AF.Copy, scale=0.125),
                waits=[("T", lastY), ("Dm", dmc), ("G", dmg)], inc=A, name="yev")
            def gns_mt(v, mt=mt):
                yv = y_ch[:, mt, :].rearrange("p (h k) -> p h k", h=H)
                v.tensor_reduce(gn_m[:, mt, :], yv, AX.X, ALU.add)
                v.tensor_mul(sq_t[:, :], y_ch[:, mt, :], y_ch[:, mt, :])
                v.drain()
                return v.tensor_reduce(gn_v[:, mt, :],
                                       sq_t[:, :].rearrange("p (h k) -> p h k", h=H),
                                       AX.X, ALU.add)
            S.step("vector", gns_mt, waits=[("A", st7["yev7"])], inc=V, name="gnstats")
        return S.cur(V)

    def gn_z_wo():
        def gfin(v):
            v.tensor_scalar_mul(gn_m[:, :, :], gn_m[:, :, :], 1.0 / 64)
            v.drain()
            v.tensor_mul(gn_r[:, :, :], gn_m[:, :, :], gn_m[:, :, :])
            v.drain()
            return v.scalar_tensor_tensor(gn_v[:, :, :], gn_v[:, :, :], 1.0 / 64,
                                          gn_r[:, :, :], ALU.mult, ALU.subtract)
        S.step("vector", gfin, waits=["V"], inc=V, name="gnfin")
        S.step("scalar", lambda s: s.activation(gn_r[:, :, :], gn_v[:, :, :], AF.Ln,
                                                bias=vecs[:, VEPS:VEPS + 1]),
               waits=[("V", S.cur(V))], inc=A, name="gnln")
        def _gnexp(s):
            s.drain()
            return s.activation(gn_r[:, :, :], gn_r[:, :, :], AF.Exp, scale=-0.5)
        ga = S.step("scalar", _gnexp, waits=[], inc=A, name="gnexp")
        tr_bank = [0, 0]
        last_zev = 0
        for mt in range(NMT):
            def nrm(v, mt=mt):
                yv = y_ch[:, mt, :].rearrange("p (h k) -> p h k", h=H)
                mb = gn_m[:, mt, :].unsqueeze(2).broadcast_to([128, H, 64])
                rb = gn_r[:, mt, :].unsqueeze(2).broadcast_to([128, H, 64])
                v.tensor_tensor(z_tok[:, :].rearrange("p (h k) -> p h k", h=H), yv, mb,
                                ALU.subtract)
                v.drain()
                return v.tensor_tensor(z_tok[:, :].rearrange("p (h k) -> p h k", h=H),
                                       z_tok[:, :].rearrange("p (h k) -> p h k", h=H),
                                       rb, ALU.mult)
            nv = S.step("vector", nrm, waits=[("A", max(ga, last_zev))], inc=V, name="gnnrm")
            for cc in range(NDC):
                par = cc % 2
                tt = S.step("tensor", lambda t, cc=cc, par=par: t.transpose(
                    pbf[2 + par][:, 0:128], z_tok[:, 128 * cc:128 * cc + 128], id_s[:, :]),
                    waits=[("V", nv), ("A", tr_bank[par])], inc=T, name="ztr")
                aa = S.step("scalar", lambda s, cc=cc, par=par, mt=mt: s.activation(
                    z3[:, cc, mt * MT:(mt + 1) * MT], pbf[2 + par][:, 0:128], AF.Identity,
                    bias=lnb_c[:, cc:cc + 1], scale=lnw_c[:, cc:cc + 1]),
                    waits=[("T", tt)], inc=A, name="zev")
                tr_bank[par] = aa
                last_zev = aa
        za = S.cur(A)
        gv = S.step("vector", lambda v: v.tensor_mul(z3[:, :, :], z3[:, :, :], g_fm[:, :, :]),
                    waits=[("A", za)], inc=V, name="zg")
        wo_slot = [S.cur(T)] * 4
        wo_bank = [0, 0]
        wo_ev = {}
        lastV = 0
        for do in range(NDC):
            sl = do % 4
            dmi = S.dma("sync", rpj[sl][:, :, :], dbo2[do], DP,
                        waits=[("T", wo_slot[sl])], name="wodma")
            tl = 0
            for cb in range(NDC):
                tl = S.step("tensor", lambda t, sl=sl, cb=cb, do=do: t.matmul(
                    pb[4 + do % 2][:, :], rpj[sl][:, cb, :], z3[:, cb, :],
                    start=(cb == 0), stop=(cb == NDC - 1)),
                    waits=[("DP", dmi), ("V", gv), ("V", wo_bank[do % 2])],
                    inc=T, name="womm")
            wo_slot[sl] = tl
            st["womm"] = tl
            lastV = S.step("vector", lambda v, do=do: v.tensor_tensor(
                resid[:, do, :], resid[:, do, :], pb[4 + do % 2][:, :], ALU.add),
                waits=[("T", tl)], inc=V, name="woev")
            wo_bank[do % 2] = lastV
            wo_ev[do] = lastV
            if do >= 2:
                ln_stats_dc(do - 2, waits=[("V", wo_ev[do - 2])])
        ln_stats_dc(NDC - 2, waits=[("V", wo_ev[NDC - 2])], on_v=True)
        ln_stats_dc(NDC - 1, waits=[("V", wo_ev[NDC - 1])], on_v=True)
        return lastV

    for c in range(nchunk + 1):
        if c > 0:
            xdmi = [0] * NDC
            def post_A(d, out_v, c=c, xdmi=xdmi):
                if c < nchunk:
                    if d >= 2:
                        ln_stats_dc(d - 2, waits=[("G", xdmi[d - 2])])
                    xdmi[d] = S.dma("gpsimd", resid[:, d, :], dx[c][:, d, :], G,
                                    waits=[("V", out_v)] +
                                          ([("G", xdmi[d - 1])] if d >= 1 else []),
                                    name="xdma")
            va = ffn_app("A", post_A)
            if c < nchunk:
                ln_stats_dc(NDC - 2, waits=[("G", xdmi[NDC - 2])], on_v=True)
                ln_stats_dc(NDC - 1, waits=[("G", xdmi[NDC - 1])], on_v=True)
            S.dma("gpsimd", dout[c - 1], stage[:, :, :], G, waits=[("V", va)], name="outdma")
        if c == nchunk:
            break
        if c == 0:
            for dcp in range(NDC):
                ln_stats_dc(dcp)
        mix_ln = do_attn and attn_upto >= 1
        bout = {}
        def post_B(d, out_v, bout=bout):
            bout[d] = out_v
            if d >= 2:
                ln_stats_dc(d - 2, waits=[("V", bout[d - 2])])
        ffn_app("B", post_B if mix_ln else None)
        if mix_ln:
            ln_stats_dc(NDC - 2, waits=[("V", bout[NDC - 2])], on_v=True)
            ln_stats_dc(NDC - 1, waits=[("V", bout[NDC - 1])], on_v=True)
        if do_attn:
            if attn_upto >= 1:
                mix_and_proj()
            if attn_upto >= 2:
                scan(scan_lvl)
            if attn_upto >= 3:
                gn_z_wo()

    S.step("sync", lambda e, v=S.cur(Dm): e.wait_ge(S.sems[Dm], v), name="finwait")
    for snm in (G, D1, D1b, D2, DP):
        S.step("sync", lambda e, snm=snm, v=S.cur(snm): e.wait_ge(S.sems[snm], v),
               name="finwx")
    S.finalize(block)
    es.close()
    return nc


def _bf(x):
    return np.ascontiguousarray(x)


def make_weight_maps(W):
    o = {}
    W1, W2 = W["L0_ffn_W1"], W["L0_ffn_W2"]
    o["w1s"] = np.ascontiguousarray(
        W1.reshape(NFC, 128, NDC, 128).transpose(0, 3, 2, 1).reshape(NFC, 128, NDC * 128))
    o["w2s"] = np.ascontiguousarray(
        W2.reshape(NDC, 128, NFC, 128).transpose(0, 3, 2, 1).reshape(NDC, 128, NFC * 128))
    for nm, ww in [("wks", W["L0_attn_Wk"]), ("wrs", W["L0_attn_Wr"]),
                   ("wgs", W["L0_attn_Wg"]), ("wos", W["L0_attn_Wo"])]:
        o[nm] = np.ascontiguousarray(
            ww.reshape(NDC, 128, NDC, 128).transpose(2, 1, 0, 3).reshape(NDC, 128, NDC * 128))
    o["wvs"] = np.ascontiguousarray(W["L0_attn_Wv"].reshape(NDC, 128, D))
    tm1 = W["L0_attn_tm_w1"]
    o["tm1a"] = np.ascontiguousarray(
        tm1[:, 0:128].reshape(NDC, 128, 128).transpose(1, 0, 2))
    o["tm1b"] = np.ascontiguousarray(
        tm1[:, 128:160].reshape(NDC, 128, 32).transpose(1, 0, 2))
    tm2 = W["L0_attn_tm_w2"]
    a = tm2[0:4].reshape(4, 32, NDC, 128).transpose(0, 1, 2, 3).reshape(128, NDC, 128)
    o["tm2"] = np.ascontiguousarray(a)
    o["tm24"] = np.ascontiguousarray(tm2[4].reshape(32, NDC, 128))
    td1 = W["L0_attn_td_w1"]
    o["td1"] = np.ascontiguousarray(td1.reshape(NDC, 128, 64).transpose(1, 0, 2))
    td2 = W["L0_attn_td_w2"]
    o["td2"] = np.ascontiguousarray(td2.reshape(64, NDC, 128))
    vec = np.zeros((128, 72), np.float32)
    for nm, off in [("x_maa", 0), ("w_maa", 8), ("k_maa", 16), ("v_maa", 24),
                    ("r_maa", 32), ("g_maa", 40)]:
        vec[:, off:off + 8] = W["L0_attn_" + nm].reshape(NDC, 128).T
    vec[:, 48:56] = W["L0_attn_time_decay"].reshape(D)[None].reshape(NDC, 128).T
    vec[:, 56:64] = W["L0_attn_time_first"].reshape(D).reshape(NDC, 128).T
    vec[:, 64] = EPS
    vec[:, 65] = 0.5
    vec[:, 66] = -0.5
    o["vecs"] = vec
    o["b1c"] = np.ascontiguousarray(W["L0_ffn_b1"].reshape(NFC, 128).T)
    o["lnwc"] = np.ascontiguousarray(W["L0_attn_ln_w"].reshape(NDC, 128).T)
    o["lnbc"] = np.ascontiguousarray(W["L0_attn_ln_b"].reshape(NDC, 128).T)
    o["b2r"] = np.ascontiguousarray(W["L0_ffn_b2"].reshape(1, D))
    ml = np.zeros((128, 128), np.uint8)
    for j in range(128):
        ml[j, j + 1:] = 1
    o["mask_lower"] = ml
    o["mask_lower_bf"] = ml.astype(ml_dtypes.bfloat16)
    o["mask_upper"] = (1 - ml).astype(np.uint8)
    o["mask_diag"] = np.eye(128).astype(np.uint8)
    o["ident"] = np.eye(128).astype(ml_dtypes.bfloat16)
    bo = np.zeros((128, 2), ml_dtypes.bfloat16)
    bo[0:64, 0] = 1.0
    bo[64:128, 1] = 1.0
    o["blockones"] = bo
    return o


def shard_x(x, core, nchunk=16):
    T = nchunk * CW
    xc = x[2 * core:2 * core + 2, :T, :]
    a = xc.reshape(2, nchunk, CW, NDC, 128)
    a = a.transpose(1, 4, 3, 0, 2)
    return np.ascontiguousarray(a.reshape(nchunk, 128, NDC, COLS))


def unshard_out(o, nchunk=16):
    a = o.reshape(nchunk, 128, NDC, 2, CW).transpose(3, 0, 4, 2, 1)
    return a.reshape(2, nchunk * CW, D)


def kernel(**inputs):
    W = inputs
    x = np.asarray(inputs["L0_attn_x"], np.float32)
    wm = make_weight_maps({k: np.asarray(v, np.float32) for k, v in inputs.items()
                           if k != "L0_attn_x"})
    nchunk = x.shape[1] // CW
    nc = build(nchunk=nchunk)
    in_maps = []
    for core in range(8):
        m = dict(wm)
        m["x"] = shard_x(x, core, nchunk)
        in_maps.append(m)
    res = run_bass_kernel_spmd(nc, in_maps, core_ids=list(range(8)))
    out = np.empty((16, x.shape[1], D), np.float32)
    for core in range(8):
        out[2 * core:2 * core + 2] = unshard_out(np.asarray(res.results[core]["out"]), nchunk)
    return out
